# version 05
# baseline (speedup 1.0000x reference)
import os
import numpy as np
import concourse.bass as bass
import concourse.mybir as mybir
from concourse.bass_utils import run_bass_kernel_spmd
from concourse.alu_op_type import AluOpType as ALU

F32 = mybir.dt.float32
AF = mybir.ActivationFunctionType
AX = mybir.AxisListType

NCORE = 8
B = 2
S = 2048
D = 1024
NP = 8
F = 4096
E = 8
DC = 8
FC = 32
TT = 4

NCOLS = 52736

C_ONES = 0
C_ID = 128
C_BQKV = 256
C_PJB = 280
C_GB1 = 288
C_B1T = 296
C_B2T = 328
C_GB2 = 336
C_TB1 = 337
C_TB2 = 593
C_GW2 = 657
C_EPS = 721
C_INV = 722
C_GB2R = 723
MISC_W = 1024

BUFA = 1024
BUFB = 17408
DYN = 33792

WB = 33792
STRIP = 36864
STRIPB = 37888
QT = 39936
KT = 41984
VT = 44032
PT = (46080, 48128)
STAGE = 50176
VBR = 52224
NMC = 52352

GBC = 52640
MB1 = 52656
MB2 = 52720

PWT = 33792
PZT = 41984
PSQ = 46176
PR_MEAN = 47200
PR_T1 = 47712
PR_T2 = 48224
PR_RSTD = 48736

GW1 = 33792
HT = 41984
GTR = 46166
TMP8 = 46176
GT2 = 46208
GT128 = 48384

MWB = (33792, 37888)
FSQ = 41984
FR_MEAN = 43008
FR_T1 = 43520
FR_T2 = 44032
FR_RSTD = 44544
OUT2 = (45056, 45568)
UF32 = (49152, 49664)

MWN = 128
FLW = (46080, 46592)
FLT = (47104, 47616, 48128)
FACC = 48640
FACCH = 50176
FACCL = 50432
XTP_ = (BUFB, BUFB + 4096)
XTH_ = (BUFB + 8192, BUFB + 10240)
XTL_ = (BUFB + 12288, BUFB + 14336)
UHI = BUFA
ULO = BUFA + 8192


class _MockInstr:
    def then_inc(self, h, k):
        return self


class _MockEng:
    def __getattr__(self, name):
        if name in ("wait_ge",):
            return lambda *a, **k: None
        return lambda *a, **k: _MockInstr()


_MOCK = _MockEng()


class Sem:
    def __init__(self, h):
        self.h = h
        self.n = 0
        self.e = 0


class Builder:
    def __init__(self, stage="full", npairs=None):
        self.stage = stage
        self.npairs = npairs
        nc = bass.Bass("TRN2", target_bir_lowering=False, debug=False)
        self.nc = nc
        self.T = {}
        self.mock = False

        self.d_xt = nc.dram_tensor("xt", [B, D, S], F32, kind="ExternalInput")
        self.d_wqkv = nc.dram_tensor("wqkv", [NP, 128, 3072], F32, kind="ExternalInput")
        self.d_pwt = nc.dram_tensor("pwt", [128, 8192], F32, kind="ExternalInput")
        self.d_gw1t = nc.dram_tensor("gw1t", [128, 8192], F32, kind="ExternalInput")
        self.d_misc = nc.dram_tensor("misc", [128, MISC_W], F32, kind="ExternalInput")
        self.d_w1t = nc.dram_tensor("w1t", [D, F], F32, kind="ExternalInput")
        self.d_tw1t = nc.dram_tensor("tw1t", [E, D, F], F32, kind="ExternalInput")
        self.d_w2t = nc.dram_tensor("w2t", [F, D], F32, kind="ExternalInput")
        self.d_tw2t = nc.dram_tensor("tw2t", [E, F, D], F32, kind="ExternalInput")
        self.d_out = nc.dram_tensor("outt", [B, D, S], F32, kind="ExternalOutput")
        mwkind = "ExternalOutput" if stage == "mw" else "Internal"
        BF16 = mybir.dt.bfloat16
        self.d_xtp = nc.dram_tensor("xtp_scr", [B, D, S], F32, kind=mwkind)
        self.d_mw1h = nc.dram_tensor("mw1h_scr", [B, D, F], BF16, kind=mwkind)
        self.d_mw1l = nc.dram_tensor("mw1l_scr", [B, D, F], BF16, kind=mwkind)
        self.d_mw2h = nc.dram_tensor("mw2h_scr", [B, DC, FC, 128, 128], BF16, kind=mwkind)
        self.d_mw2l = nc.dram_tensor("mw2l_scr", [B, DC, FC, 128, 128], BF16, kind=mwkind)
        if stage in ("attn", "ln1", "ln1x", "probe", "gate", "fulldbg"):
            self.d_dbg = nc.dram_tensor("dbg", [B, 128, 16384], F32, kind="ExternalOutput")

    def ar2(self, p0, np_, col, w):
        return bass.AP(self.AR, p0 * NCOLS + col, [[NCOLS, np_], [1, w]])

    def ar3(self, col, dims):
        return bass.AP(self.AR, col, [[NCOLS, 128]] + dims)

    def psa(self, p0, np_, col, w):
        return bass.AP(self.PSA, p0 * 2048 + col, [[2048, np_], [1, w]])

    def psb(self, p0, np_, col, w):
        return bass.AP(self.PSB, p0 * 2048 + col, [[2048, np_], [1, w]])

    def tag(self, key, sem, instr, k=1):
        if self.mock:
            sem.n += k
            assert key not in self.T, key
            self.T[key] = (sem, sem.n)
        else:
            sem.e += k
            assert self.T[key] == (sem, sem.e), (key, self.T[key], sem.e)
            instr.then_inc(sem.h, k)

    def tagn(self, sem, instr, k=1):
        if self.mock:
            sem.n += k
        else:
            sem.e += k
            instr.then_inc(sem.h, k)

    def alias(self, key2, sem):
        if self.mock:
            self.T[key2] = (sem, sem.n)

    def wt(self, eng, key):
        if self.mock:
            return
        sem, t = self.T[key]
        if id(sem) in self.dma_sems:
            sv = self.sealed.get(id(sem), 0)
            if sv >= t:
                t = sv
        eng.wait_ge(sem.h, t)

    def emit_block(self, sections):
        self.sealed = {id(sem): sem.n for sem in self.sems.values()}
        self.mock = True
        for _, fn in sections:
            fn(_MOCK)
        self.mock = False
        with self.nc.Block() as blk:
            for name, fn in sections:
                getattr(blk, name)(fn)

    def build(self):
        nc = self.nc
        import contextlib
        with contextlib.ExitStack() as ctx:
            AR = ctx.enter_context(nc.sbuf_tensor("AR", [128, NCOLS], F32))
            PSA = ctx.enter_context(nc.psum_tensor("PSA", [128, 2048], F32))
            PSB = ctx.enter_context(nc.psum_tensor("PSB", [128, 2048], F32))
            self.AR, self.PSA, self.PSB = AR, PSA, PSB
            names = ["SG", "SD", "MM", "AC", "DV", "MISC", "XT", "WBS", "PWT",
                     "GW1", "XTPL", "MW1L0", "MW1L1", "MW2L0", "MW2L1",
                     "LW0", "LW1", "LT0", "LT1", "LT2"]
            sems = {n: Sem(ctx.enter_context(nc.semaphore(n))) for n in names}
            self.sems = sems
            self.SG, self.SD = sems["SG"], sems["SD"]
            self.MM, self.AC, self.DV = sems["MM"], sems["AC"], sems["DV"]
            self.S_MISC, self.S_XT, self.S_WB = sems["MISC"], sems["XT"], sems["WBS"]
            self.S_PWT, self.S_GW1, self.S_XTPL = sems["PWT"], sems["GW1"], sems["XTPL"]
            self.S_MW1L = (sems["MW1L0"], sems["MW1L1"])
            self.S_MW2L = (sems["MW2L0"], sems["MW2L1"])
            self.S_LW = (sems["LW0"], sems["LW1"])
            self.S_LT = (sems["LT0"], sems["LT1"], sems["LT2"])
            self.dma_sems = {id(sems[n]) for n in names if n not in ("MM", "AC", "DV")}
            self.sealed = {}

            self.prologue()
            if self.stage == "ln1x":
                for p in range(NP):
                    self.a_pair(0, p)
                self.a_fin(0)
                for p in range(NP):
                    self.a_pair(1, p)
                self.a_fin(1)
                self.p_phase(1)
                self.dbg_dump(1, "ln1")
                self.epilogue()
                return nc
            if self.stage == "probe":
                for p in range(NP if self.npairs is None else self.npairs):
                    self.a_pair(0, p)
                self.probe_dump()
                self.epilogue()
                return nc
            for b in range(B):
                for p in range(NP):
                    self.a_pair(b, p)
                self.a_fin(b)
                if self.stage == "attn":
                    self.dbg_dump(b, "attn")
                    continue
                self.p_phase(b)
                if self.stage == "ln1":
                    self.dbg_dump(b, "ln1")
                    continue
                self.g_phase(b)
            if self.stage == "gate":
                self.gate_dump()
            if self.stage == "fulldbg":
                self.gate_dump(slot=1)
            if self.stage not in ("attn", "ln1", "gate"):
                self.mw_block(0)
                if self.stage == "mw":
                    self.mw_block(1)
                    self.mw_wait()
                    self.epilogue()
                    return nc
                prev = [None]
                for b in range(B):
                    for tt in range(TT):
                        self.f_block(b, tt, prev[0])
                        prev[0] = (b, tt)
                if self.stage == "fulldbg":
                    self.gate_dump()
            self.epilogue()
        return nc

    def prologue(self):
        def f_sync(eng):
            self.tag(("misc",), self.S_MISC, eng.dma_start(
                out=self.ar2(0, 128, 0, MISC_W),
                in_=bass.AP(self.d_misc, 0, [[MISC_W, 128], [1, MISC_W]])), 16)
            self.tag(("xt", 0), self.S_XT, eng.dma_start(
                out=self.ar3(BUFA, [[2048, DC], [1, 2048]]),
                in_=bass.AP(self.d_xt, 0, [[S, 128], [128 * S, DC], [1, S]])), 16)
            self.tag(("wb", 0, 0), self.S_WB, eng.dma_start(
                out=self.ar2(0, 128, WB, 3072),
                in_=bass.AP(self.d_wqkv, 0, [[3072, 128], [1, 3072]])), 16)

        self.emit_block([("sync", f_sync)])

    def a_pair(self, b, p):
        AR = self.AR

        def prev_it(h, it, back=1):
            g = h * 16 + it - back
            if g < 0:
                return None
            return (g // 16, g % 16)

        def f_sync(eng):
            if b > 0 and p == 0:
                self.tag(("wb", b, 0), self.S_WB, eng.dma_start(
                    out=self.ar2(0, 128, WB, 3072),
                    in_=bass.AP(self.d_wqkv, 0, [[3072, 128], [1, 3072]])), 16)
            if p + 1 < NP:
                self.wt(eng, ("vmm", b, p, 15))
                self.tag(("wb", b, p + 1), self.S_WB, eng.dma_start(
                    out=self.ar2(0, 128, WB, 3072),
                    in_=bass.AP(self.d_wqkv, (p + 1) * 128 * 3072,
                                [[3072, 128], [1, 3072]])), 16)

        def f_pe(eng):
            self.wt(eng, ("wb", b, p))
            self.wt(eng, ("misc",))
            self.wt(eng, ("xt", b))
            if ("exp", b, p - 1, 1, 15) in self.T:
                self.wt(eng, ("exp", b, p - 1, 1, 15))
                self.wt(eng, ("evB", b, p - 1, 1, 15))
                self.wt(eng, ("oev", b, p - 1, 1, 15))
            elif ("exp", b - 1, NP - 1, 1, 15) in self.T:
                self.wt(eng, ("exp", b - 1, NP - 1, 1, 15))
                self.wt(eng, ("evB", b - 1, NP - 1, 1, 15))
                self.wt(eng, ("oev", b - 1, NP - 1, 1, 15))

            tp = eng.transpose(self.psb(0, 1, 0, 128),
                               AR[0:128, C_BQKV + 3 * p + 2: C_BQKV + 3 * p + 3],
                               AR[0:128, C_ID: C_ID + 128])
            self.tag(("vbrow_mm", b, p), self.MM, tp)

            for role, colw, bank0 in (("q", 0, 0), ("k", 128, 1024)):
                for tt in range(TT):
                    if tt >= 2:
                        self.wt(eng, (role + "ev", b, p, tt - 2))
                    ps = self.psa(0, 128, bank0 + (tt % 2) * 512, 512)
                    for dc in range(DC):
                        ins = eng.matmul(
                            ps,
                            AR[0:128, WB + dc * 384 + colw: WB + dc * 384 + colw + 128],
                            AR[0:128, BUFA + dc * 2048 + tt * 512: BUFA + dc * 2048 + tt * 512 + 512],
                            start=(dc == 0), stop=(dc == DC - 1))
                    self.tag((role + "mm", b, p, tt), self.MM, ins)

            self.wt(eng, ("vbrow", b, p))
            for t2c in range(16):
                if t2c >= 4:
                    self.wt(eng, ("vev", b, p, t2c - 4))
                ps = self.psb(0, 128, (t2c % 4) * 512, 128)
                for dc in range(DC):
                    eng.matmul(
                        ps,
                        AR[0:128, BUFA + dc * 2048 + t2c * 128: BUFA + dc * 2048 + t2c * 128 + 128],
                        AR[0:128, WB + dc * 384 + 256: WB + dc * 384 + 384],
                        start=(dc == 0), stop=False)
                ins = eng.matmul(ps, AR[0:1, C_ONES: C_ONES + 128],
                                 AR[0:1, VBR: VBR + 128], start=False, stop=True)
                self.tag(("vmm", b, p, t2c), self.MM, ins)

            for h in range(2):
                base = h * 64
                for it in range(17):
                    k = it - 1
                    if it >= 1:
                        par = k % 2
                        self.wt(eng, ("norm", b, p, h, k))
                        if (h, k) == (0, 0):
                            self.wt(eng, ("vev", b, p, 15))
                        pv1b = prev_it(h, k, 1)
                        if pv1b is not None:
                            self.wt(eng, ("evB", b, p, pv1b[0], pv1b[1]))
                        for j in range(8):
                            ins = eng.matmul(
                                self.psb(0, 128, j * 128, 128),
                                AR[0:128, PT[par] + j * 128: PT[par] + j * 128 + 128],
                                AR[0:128, C_ID: C_ID + 128],
                                start=(j % 4 == 0), stop=(j % 4 == 3),
                                is_transpose=True)
                        self.tag(("tA", b, p, h, k), self.MM, ins)
                    if it < 16:
                        pv = prev_it(h, it)
                        if pv is None:
                            self.wt(eng, ("kev", b, p, TT - 1))
                            self.wt(eng, ("vev", b, p, 15))
                        else:
                            self.wt(eng, ("exp", b, p, pv[0], pv[1]))
                        for ns in range(4):
                            ins = eng.matmul(
                                self.psa(0, 128, ns * 512, 512),
                                AR[base: base + 64, QT + it * 128: QT + it * 128 + 128],
                                AR[base: base + 64, KT + ns * 512: KT + ns * 512 + 512],
                                start=True, stop=True)
                            self.tag(("smmp", b, p, h, it, ns), self.MM, ins)
                        self.alias(("smm", b, p, h, it), self.MM)
                    if it >= 1:
                        par = k % 2
                        self.wt(eng, ("evA", b, p, h, k))
                        for j in range(8, 16):
                            col = 1536 + (j - 8) * 128 if j < 12 else (j - 12) * 128
                            ins = eng.matmul(
                                self.psb(0, 128, col, 128),
                                AR[0:128, PT[par] + j * 128: PT[par] + j * 128 + 128],
                                AR[0:128, C_ID: C_ID + 128],
                                start=(j % 4 == 0), stop=(j % 4 == 3),
                                is_transpose=True)
                        self.tag(("tB", b, p, h, k), self.MM, ins)
                        pv1 = prev_it(h, k, 1)
                        if pv1 is not None:
                            self.wt(eng, ("oev", b, p, pv1[0], pv1[1]))
                        for t2c in range(8):
                            ins = eng.matmul(
                                bass.AP(self.PSB, 1024, [[2048, 64], [1, 128]]),
                                AR[0:128, VT + t2c * 128 + base: VT + t2c * 128 + base + 64],
                                AR[0:128, STRIP + t2c * 128: STRIP + t2c * 128 + 128],
                                start=(t2c == 0), stop=False)
                        self.wt(eng, ("evB", b, p, h, k))
                        for t2c in range(8, 16):
                            ins = eng.matmul(
                                bass.AP(self.PSB, 1024, [[2048, 64], [1, 128]]),
                                AR[0:128, VT + t2c * 128 + base: VT + t2c * 128 + base + 64],
                                AR[0:128, STRIPB + (t2c - 8) * 128: STRIPB + (t2c - 8) * 128 + 128],
                                start=False, stop=(t2c == 15))
                        self.tag(("pvB", b, p, h, k), self.MM, ins)

        def f_act(eng):
            for h in range(2):
                for it in range(17):
                    k = it - 1
                    if it >= 1:
                        self.wt(eng, ("tA", b, p, h, k))
                        ins = eng.activation(
                            out=AR[0:128, STRIP: STRIP + 1024],
                            in_=self.psb(0, 128, 0, 1024),
                            func=AF.Copy, bias=0.0, scale=1.0)
                        self.tag(("evA", b, p, h, k), self.AC, ins)
                        self.wt(eng, ("tB", b, p, h, k))
                        eng.activation(
                            out=AR[0:128, STRIPB: STRIPB + 512],
                            in_=self.psb(0, 128, 1536, 512),
                            func=AF.Copy, bias=0.0, scale=1.0)
                        ins = eng.activation(
                            out=AR[0:128, STRIPB + 512: STRIPB + 1024],
                            in_=self.psb(0, 128, 0, 512),
                            func=AF.Copy, bias=0.0, scale=1.0)
                        self.tag(("evB", b, p, h, k), self.AC, ins)
                    if it < 16:
                        self.wt(eng, ("smm", b, p, h, it))
                        self.wt(eng, ("bias", b, p, h, it))
                        ins = eng.activation(
                            out=AR[0:128, PT[it % 2]: PT[it % 2] + 2048],
                            in_=self.psa(0, 128, 0, 2048),
                            func=AF.Exp,
                            bias=AR[0:128, NMC + 1: NMC + 2],
                            scale=-1.0,
                            accum_out=AR[0:128, NMC + 2: NMC + 3])
                        self.tag(("exp", b, p, h, it), self.AC, ins)
                        self.tag(("sums", b, p, h, it), self.AC, eng.drain())

        def f_vec(eng):
            self.wt(eng, ("vbrow_mm", b, p))
            ins = eng.tensor_copy(AR[0:1, VBR: VBR + 128], self.psb(0, 1, 0, 128))
            self.tag(("vbrow", b, p), self.DV, ins)
            for role, bank0, bcol in (("q", 0, 0), ("k", 1024, 1)):
                dstc = QT if role == "q" else KT
                for tt in range(TT):
                    self.wt(eng, (role + "mm", b, p, tt))
                    if role == "q":
                        ins = eng.tensor_scalar(
                            out=AR[0:128, dstc + tt * 512: dstc + tt * 512 + 512],
                            in0=self.psa(0, 128, bank0 + (tt % 2) * 512, 512),
                            scalar1=AR[0:128, C_BQKV + 3 * p + bcol: C_BQKV + 3 * p + bcol + 1],
                            scalar2=-0.125, op0=ALU.add, op1=ALU.mult)
                    else:
                        ins = eng.tensor_scalar_add(
                            AR[0:128, dstc + tt * 512: dstc + tt * 512 + 512],
                            self.psa(0, 128, bank0 + (tt % 2) * 512, 512),
                            AR[0:128, C_BQKV + 3 * p + bcol: C_BQKV + 3 * p + bcol + 1])
                    self.tag((role + "ev", b, p, tt), self.DV, ins)
            for t2c in range(16):
                self.wt(eng, ("vmm", b, p, t2c))
                ins = eng.tensor_copy(
                    AR[0:128, VT + t2c * 128: VT + t2c * 128 + 128],
                    self.psb(0, 128, (t2c % 4) * 512, 128))
                self.tag(("vev", b, p, t2c), self.DV, ins)
            for h in range(2):
                for it in range(17):
                    if it < 16:
                        for ns in range(4):
                            self.wt(eng, ("smmp", b, p, h, it, ns))
                            eng.tensor_reduce(
                                AR[0:128, NMC + 4 + ns: NMC + 5 + ns],
                                self.psa(0, 128, ns * 512, 512),
                                axis=AX.X, op=ALU.min)
                        eng.drain()
                        eng.tensor_reduce(
                            AR[0:128, NMC + 1: NMC + 2],
                            AR[0:128, NMC + 4: NMC + 8],
                            axis=AX.X, op=ALU.min)
                        self.tag(("bias", b, p, h, it), self.DV, eng.drain())
                        self.wt(eng, ("sums", b, p, h, it))
                        eng.reciprocal(AR[0:128, NMC + 3: NMC + 4],
                                       AR[0:128, NMC + 2: NMC + 3])
                        eng.drain()
                        ins = eng.tensor_scalar_mul(
                            AR[0:128, PT[it % 2]: PT[it % 2] + 2048],
                            AR[0:128, PT[it % 2]: PT[it % 2] + 2048],
                            AR[0:128, NMC + 3: NMC + 4])
                        self.tag(("norm", b, p, h, it), self.DV, ins)
                    if it >= 1:
                        k = it - 1
                        self.wt(eng, ("pvB", b, p, h, k))
                        if h == 1 and k == 0:
                            if ("oshift", b, p - 1) in self.T:
                                self.wt(eng, ("oshift", b, p - 1))
                            elif ("oshift", b - 1, NP - 1) in self.T:
                                self.wt(eng, ("oshift", b - 1, NP - 1))
                        if h == 0:
                            ins = eng.tensor_copy(
                                AR[0:64, BUFB + p * 2048 + k * 128: BUFB + p * 2048 + k * 128 + 128],
                                bass.AP(self.PSB, 1024, [[2048, 64], [1, 128]]))
                        else:
                            ins = eng.tensor_copy(
                                AR[0:64, STAGE + k * 128: STAGE + k * 128 + 128],
                                bass.AP(self.PSB, 1024, [[2048, 64], [1, 128]]))
                        self.tag(("oev", b, p, h, k), self.DV, ins)

        def f_gps(eng):
            self.wt(eng, ("oev", b, p, 1, 15))
            ins = eng.dma_start(
                out=AR[64:128, BUFB + p * 2048: BUFB + p * 2048 + 2048],
                in_=AR[0:64, STAGE: STAGE + 2048])
            self.tag(("oshift", b, p), self.SG, ins, 16)

        self.emit_block([("sync", f_sync), ("tensor", f_pe), ("scalar", f_act),
                         ("vector", f_vec), ("gpsimd", f_gps)])

    def a_fin(self, b):
        AR = self.AR

        def f_sync(eng):
            if self.stage != "attn":
                self.tag(("pwt", b), self.S_PWT, eng.dma_start(
                    out=self.ar2(0, 128, PWT, 8192),
                    in_=bass.AP(self.d_pwt, 0, [[8192, 128], [1, 8192]])), 16)
                if self.stage == "ln1x" and b == 0:
                    self.tag(("xt", 1), self.S_XT, eng.dma_start(
                        out=self.ar3(BUFA, [[2048, DC], [1, 2048]]),
                        in_=bass.AP(self.d_xt, D * S, [[S, 128], [128 * S, DC], [1, S]])), 16)
            elif b == 0:
                self.tag(("xt", 1), self.S_XT, eng.dma_start(
                    out=self.ar3(BUFA, [[2048, DC], [1, 2048]]),
                    in_=bass.AP(self.d_xt, D * S, [[S, 128], [128 * S, DC], [1, S]])), 16)

        self.emit_block([("sync", f_sync)])

    def p_phase(self, b):
        AR = self.AR

        def f_pe(eng):
            self.wt(eng, ("pwt", b))
            for pp in range(NP):
                self.wt(eng, ("oev", b, pp, 0, 15))
                self.wt(eng, ("oshift", b, pp))
            for tt in range(TT):
                for dc in range(DC):
                    k = tt * DC + dc
                    if k >= 2:
                        self.wt(eng, ("zev", b, k - 2))
                    ps = self.psa(0, 128, (dc % 2) * 512, 512)
                    for pc in range(NP):
                        ins = eng.matmul(
                            ps,
                            AR[0:128, PWT + pc * 1024 + dc * 128: PWT + pc * 1024 + dc * 128 + 128],
                            AR[0:128, BUFB + pc * 2048 + tt * 512: BUFB + pc * 2048 + tt * 512 + 512],
                            start=(pc == 0), stop=(pc == NP - 1))
                    self.tag(("ymm", b, k), self.MM, ins)
                for dc in range(DC):
                    self.wt(eng, ("zev", b, tt * DC + dc))
                    eng.matmul(
                        self.psb(0, 1, 0, 512),
                        AR[0:128, C_ONES: C_ONES + 1],
                        AR[0:128, PZT + dc * 512: PZT + dc * 512 + 512],
                        start=(dc == 0), stop=(dc == DC - 1))
                    self.wt(eng, ("sq", b, tt, dc))
                    ins2 = eng.matmul(
                        self.psb(0, 1, 512, 512),
                        AR[0:128, C_ONES: C_ONES + 1],
                        AR[0:128, PSQ + (dc % 2) * 512: PSQ + (dc % 2) * 512 + 512],
                        start=(dc == 0), stop=(dc == DC - 1))
                    self.tag(("ssq", b, tt, dc), self.MM, ins2)
                self.wt(eng, ("rows", b, tt))
                eng.matmul(self.psb(0, 128, 1024, 512),
                           AR[0:1, C_ONES: C_ONES + 128],
                           AR[0:1, PR_MEAN: PR_MEAN + 512], start=True, stop=True)
                i2 = eng.matmul(self.psb(0, 128, 1536, 512),
                                AR[0:1, C_ONES: C_ONES + 128],
                                AR[0:1, PR_RSTD: PR_RSTD + 512], start=True, stop=True)
                self.tag(("bcln", b, tt), self.MM, i2)

        def f_vec(eng):
            for tt in range(TT):
                for dc in range(DC):
                    k = tt * DC + dc
                    self.wt(eng, ("ymm", b, k))
                    if tt > 0:
                        self.wt(eng, ("sq", b, tt - 1, dc))
                    ins = eng.scalar_tensor_tensor(
                        out=AR[0:128, PZT + dc * 512: PZT + dc * 512 + 512],
                        in0=self.psa(0, 128, (dc % 2) * 512, 512),
                        scalar=AR[0:128, C_PJB + dc: C_PJB + dc + 1],
                        in1=AR[0:128, BUFA + dc * 2048 + tt * 512: BUFA + dc * 2048 + tt * 512 + 512],
                        op0=ALU.add, op1=ALU.add)
                    self.tag(("zev", b, k), self.DV, ins)
                self.wt(eng, ("ssq", b, tt, DC - 1))
                eng.tensor_scalar_mul(AR[0:1, PR_MEAN: PR_MEAN + 512],
                                      self.psb(0, 1, 0, 512), 1.0 / 1024.0)
                eng.tensor_scalar_mul(AR[0:1, PR_T1: PR_T1 + 512],
                                      self.psb(0, 1, 512, 512), 1.0 / 1024.0)
                eng.tensor_tensor(out=AR[0:1, PR_T2: PR_T2 + 512],
                                  in0=AR[0:1, PR_MEAN: PR_MEAN + 512],
                                  in1=AR[0:1, PR_MEAN: PR_MEAN + 512], op=ALU.mult)
                ins = eng.tensor_tensor(out=AR[0:1, PR_T1: PR_T1 + 512],
                                        in0=AR[0:1, PR_T1: PR_T1 + 512],
                                        in1=AR[0:1, PR_T2: PR_T2 + 512], op=ALU.subtract)
                self.tag(("var", b, tt), self.DV, ins)
                self.wt(eng, ("sdrow", b, tt))
                eng.reciprocal(AR[0:1, PR_RSTD: PR_RSTD + 512],
                               AR[0:1, PR_T2: PR_T2 + 512])
                self.tag(("rows", b, tt), self.DV, eng.drain())
                self.wt(eng, ("bcln", b, tt))
                for dc in range(DC):
                    eng.tensor_tensor(
                        out=AR[0:128, BUFB + dc * 2048 + tt * 512: BUFB + dc * 2048 + tt * 512 + 512],
                        in0=AR[0:128, PZT + dc * 512: PZT + dc * 512 + 512],
                        in1=self.psb(0, 128, 1024, 512), op=ALU.subtract)
                    ins = eng.tensor_tensor(
                        out=AR[0:128, BUFB + dc * 2048 + tt * 512: BUFB + dc * 2048 + tt * 512 + 512],
                        in0=AR[0:128, BUFB + dc * 2048 + tt * 512: BUFB + dc * 2048 + tt * 512 + 512],
                        in1=self.psb(0, 128, 1536, 512), op=ALU.mult)
                self.tag(("xtp", b, tt), self.DV, ins)

        def f_act(eng):
            for tt in range(TT):
                for dc in range(DC):
                    self.wt(eng, ("zev", b, tt * DC + dc))
                    if tt * DC + dc >= 2:
                        self.wt(eng, ("ssq", b, tt, dc - 2) if dc >= 2
                                else ("ssq", b, tt - 1, DC + dc - 2))
                    ins = eng.activation(
                        out=AR[0:128, PSQ + (dc % 2) * 512: PSQ + (dc % 2) * 512 + 512],
                        in_=AR[0:128, PZT + dc * 512: PZT + dc * 512 + 512],
                        func=AF.Square, bias=0.0, scale=1.0)
                    self.tag(("sq", b, tt, dc), self.AC, ins)
                self.wt(eng, ("var", b, tt))
                ins = eng.activation(out=AR[0:1, PR_T2: PR_T2 + 512],
                                     in_=AR[0:1, PR_T1: PR_T1 + 512],
                                     func=AF.Sqrt,
                                     bias=AR[0:1, C_EPS: C_EPS + 1], scale=1.0)
                self.tag(("sdrow", b, tt), self.AC, ins)

        def f_gps(eng):
            for tt in range(TT):
                self.wt(eng, ("xtp", b, tt))
                ins = eng.dma_start(
                    out=bass.AP(self.d_xtp, b * D * S + tt * 512,
                                [[S, 128], [128 * S, DC], [1, 512]]),
                    in_=bass.AP(AR, BUFB + tt * 512, [[NCOLS, 128], [2048, DC], [1, 512]]))
                self.tag(("xtpw", b, tt), self.SG, ins, 16)

        self.emit_block([("tensor", f_pe), ("vector", f_vec), ("scalar", f_act),
                         ("gpsimd", f_gps)])

    def g_phase(self, b):
        AR = self.AR

        def f_sync(eng):
            self.tag(("gw1", b), self.S_GW1, eng.dma_start(
                out=self.ar2(0, 128, GW1, 8192),
                in_=bass.AP(self.d_gw1t, 0, [[8192, 128], [1, 8192]])), 16)
            if b == 0:
                self.tag(("xt", 1), self.S_XT, eng.dma_start(
                    out=self.ar3(BUFA, [[2048, DC], [1, 2048]]),
                    in_=bass.AP(self.d_xt, D * S, [[S, 128], [128 * S, DC], [1, S]])), 16)

        def f_pe(eng):
            self.wt(eng, ("gw1", b))
            for tt in range(TT):
                for oc in range(DC):
                    k = tt * DC + oc
                    if k >= 2:
                        self.wt(eng, ("hev", b, k - 2))
                    ps = self.psa(0, 128, (oc % 2) * 512, 512)
                    for dc in range(DC):
                        ins = eng.matmul(
                            ps,
                            AR[0:128, GW1 + dc * 1024 + oc * 128: GW1 + dc * 1024 + oc * 128 + 128],
                            AR[0:128, BUFB + dc * 2048 + tt * 512: BUFB + dc * 2048 + tt * 512 + 512],
                            start=(dc == 0), stop=(dc == DC - 1))
                    self.tag(("hmm", b, k), self.MM, ins)
                for oc in range(DC):
                    self.wt(eng, ("hev", b, tt * DC + oc))
                    ins = eng.matmul(
                        self.psb(0, 8, tt * 512, 512),
                        AR[0:128, C_GW2 + oc * 8: C_GW2 + oc * 8 + 8],
                        AR[0:128, HT + oc * 512: HT + oc * 512 + 512],
                        start=(oc == 0), stop=(oc == DC - 1))
                self.tag(("g2", b, tt), self.MM, ins)
            self.wt(eng, ("gevac", b))
            for c in range(16):
                ins = eng.matmul(self.psa(0, 128, 1024, 8),
                                 AR[0:8, GT2 + c * 128: GT2 + c * 128 + 128],
                                 AR[0:8, C_ID: C_ID + 8],
                                 start=(c == 0), stop=(c == 15), is_transpose=True)
            self.tag(("gtacc", b), self.MM, ins)
            self.wt(eng, ("g128", b))
            eng.matmul(self.psa(0, 1, 1536, 8),
                       AR[0:128, C_INV: C_INV + 1],
                       AR[0:128, GT128: GT128 + 8], start=True, stop=False)
            ins = eng.matmul(self.psa(0, 1, 1536, 8),
                             AR[0:1, C_ONES: C_ONES + 1],
                             AR[0:1, C_GB2R: C_GB2R + 8], start=False, stop=True)
            self.tag(("growmm", b), self.MM, ins)
            self.wt(eng, ("gtrow", b))
            ins = eng.matmul(self.psb(0, 128, 0, 8),
                             AR[0:1, C_ONES: C_ONES + 128],
                             AR[0:1, GTR: GTR + 8], start=True, stop=True)
            self.tag(("gbcmm", b), self.MM, ins)

        def f_act(eng):
            for tt in range(TT):
                for oc in range(DC):
                    k = tt * DC + oc
                    self.wt(eng, ("hmm", b, k))
                    ins = eng.activation(
                        out=AR[0:128, HT + oc * 512: HT + oc * 512 + 512],
                        in_=self.psa(0, 128, (oc % 2) * 512, 512),
                        func=AF.Relu, bias=AR[0:128, C_GB1 + oc: C_GB1 + oc + 1],
                        scale=1.0)
                    self.tag(("hev", b, k), self.AC, ins)
            for tt in range(TT):
                self.wt(eng, ("g2", b, tt))
            ins = eng.activation(out=AR[0:8, GT2: GT2 + 2048],
                                 in_=self.psb(0, 8, 0, 2048),
                                 func=AF.Copy, bias=0.0, scale=1.0)
            self.tag(("gevac", b), self.AC, ins)

        def f_vec(eng):
            self.wt(eng, ("gtacc", b))
            eng.tensor_copy(AR[0:128, GT128: GT128 + 8],
                            self.psa(0, 128, 1024, 8))
            self.tag(("g128", b), self.DV, eng.drain())
            self.wt(eng, ("growmm", b))
            eng.tensor_copy(AR[0:1, GTR: GTR + 8], self.psa(0, 1, 1536, 8))
            self.tag(("gtrow", b), self.DV, eng.drain())
            self.wt(eng, ("gbcmm", b))
            eng.tensor_copy(AR[0:128, GBC + 8 * b: GBC + 8 * b + 8],
                            self.psb(0, 128, 0, 8))
            self.tag(("gbc", b), self.DV, eng.drain())
            for nchunk, ctb, cbt, dst in ((32, C_TB1, C_B1T, MB1 + 32 * b),
                                          (8, C_TB2, C_B2T, MB2 + 8 * b)):
                for c in range(nchunk):
                    for e in range(E):
                        ins = eng.scalar_tensor_tensor(
                            out=AR[0:128, dst + c: dst + c + 1],
                            in0=AR[0:128, ctb + c * 8 + e: ctb + c * 8 + e + 1],
                            scalar=bass.AP(self.PSB, e, [[2048, 128], [1, 1]]),
                            in1=(AR[0:128, cbt + c: cbt + c + 1] if e == 0
                                 else AR[0:128, dst + c: dst + c + 1]),
                            op0=ALU.mult, op1=ALU.add)
            self.tag(("mbdone", b), self.DV, ins)

        self.emit_block([("sync", f_sync) if b == 0 else ("sync", f_sync),
                         ("tensor", f_pe), ("scalar", f_act), ("vector", f_vec)])

    def mw_unit(self, gi):
        kind_units = [("w1", dc, ch) for dc in range(DC) for ch in range(8)] + \
                     [("w2", fc, ch) for fc in range(FC) for ch in range(2)]
        return kind_units[gi % MWN]

    def mw_load(self, eng, gi):
        kind, i, ch = self.mw_unit(gi)
        if kind == "w1":
            off, pitch = i * 128 * F + ch * 512, F
            dw, dtw, tstep = self.d_w1t, self.d_tw1t, D * F
        else:
            off, pitch = i * 128 * D + ch * 512, D
            dw, dtw, tstep = self.d_w2t, self.d_tw2t, F * D
        if gi >= 2:
            self.wt(eng, ("mst", gi - 2, 0))
        self.tag(("mlw", gi), self.S_LW[gi % 2], eng.dma_start(
            out=self.ar2(0, 128, FLW[gi % 2], 512),
            in_=bass.AP(dw, off, [[pitch, 128], [1, 512]])), 16)
        for e in range(E):
            if e >= 3:
                self.wt(eng, ("mst", gi, e - 3))
            elif gi >= 1:
                self.wt(eng, ("mst", gi - 1, (6, 7, 5)[e]))
            self.tag(("mlt", gi, e), self.S_LT[e % 3], eng.dma_start(
                out=self.ar2(0, 128, FLT[e % 3], 512),
                in_=bass.AP(dtw, e * tstep + off, [[pitch, 128], [1, 512]])), 16)

    def mw_stt(self, eng, gi):
        AR = self.AR
        bb = gi // MWN
        if gi % MWN == 0:
            self.wt(eng, ("gbc", bb))
        self.wt(eng, ("mlw", gi))
        for e in range(E):
            self.wt(eng, ("mlt", gi, e))
            ins = eng.scalar_tensor_tensor(
                out=AR[0:128, FACC: FACC + 512],
                in0=AR[0:128, FLT[e % 3]: FLT[e % 3] + 512],
                scalar=AR[0:128, GBC + 8 * bb + e: GBC + 8 * bb + e + 1],
                in1=(AR[0:128, FLW[gi % 2]: FLW[gi % 2] + 512] if e == 0
                     else AR[0:128, FACC: FACC + 512]),
                op0=ALU.mult, op1=ALU.add)
            self.tag(("mst", gi, e), self.DV, ins)
        if gi >= 1:
            self.wt(eng, ("mwr", gi - 1))
        BF16 = mybir.dt.bfloat16
        hiv = AR[0:128, FACCH: FACCH + 256].bitcast(BF16)
        lov = AR[0:128, FACCL: FACCL + 256].bitcast(BF16)
        eng.tensor_copy(hiv, AR[0:128, FACC: FACC + 512])
        eng.tensor_tensor(out=lov, in0=AR[0:128, FACC: FACC + 512], in1=hiv,
                          op=ALU.subtract)
        self.tagn(self.DV, eng.drain())
        self.alias(("macc", gi), self.DV)

    def mw_write(self, eng, gi):
        BF16 = mybir.dt.bfloat16
        bb = gi // MWN
        kind, i, ch = self.mw_unit(gi)
        self.wt(eng, ("macc", gi))
        for dmat1, dmat2, reg in ((self.d_mw1h, self.d_mw2h, FACCH),
                                  (self.d_mw1l, self.d_mw2l, FACCL)):
            if kind == "w1":
                ins = eng.dma_start(
                    out=bass.AP(dmat1, bb * D * F + i * 128 * F + ch * 512,
                                [[F, 128], [1, 512]]),
                    in_=bass.AP(self.AR, reg,
                                [[NCOLS, 128], [1, 256]]).bitcast(BF16))
            else:
                ins = eng.dma_start(
                    out=bass.AP(dmat2, ((bb * DC + ch * 4) * FC + i) * 16384,
                                [[128, 128], [FC * 16384, 4], [1, 128]]),
                    in_=bass.AP(self.AR, reg,
                                [[NCOLS, 128], [64, 4], [1, 64]]).bitcast(BF16))
            self.tagn(self.SD, ins, 16)
        self.alias(("mwr", gi), self.SD)
        if gi % MWN == MWN - 1:
            self.alias(("mwall", bb), self.SD)

    def mw_block(self, bb):
        def f_sync(eng):
            for ui in range(MWN):
                self.mw_load(eng, bb * MWN + ui)

        def f_vec(eng):
            for ui in range(MWN):
                self.mw_stt(eng, bb * MWN + ui)

        def f_act(eng):
            for ui in range(MWN):
                self.mw_write(eng, bb * MWN + ui)

        self.emit_block([("sync", f_sync), ("vector", f_vec), ("scalar", f_act)])

    def f_block(self, b, tt, prev):
        AR = self.AR
        BF16 = mybir.dt.bfloat16
        xtp = XTP_[tt % 2]
        xth = XTH_[tt % 2]
        xtl = XTL_[tt % 2]

        def bfv(col, w32, dims=None):
            if dims is None:
                return bass.AP(AR, col, [[NCOLS, 128], [1, w32]]).bitcast(BF16)
            return bass.AP(AR, col, [[NCOLS, 128]] + dims).bitcast(BF16)

        def f_sync(eng):
            if tt == 0:
                self.wt(eng, ("mwall", b))
            self.wt(eng, ("xtpw", b, tt))
            pk = None
            if tt >= 2:
                pk = (b, tt - 2)
            elif b == 1:
                pk = (0, tt + 2)
            if pk is not None:
                self.wt(eng, ("outr", pk[0], pk[1]))
                self.wt(eng, ("fsd", pk[0], pk[1]))
                self.wt(eng, ("f1", pk[0], pk[1], 7))
            self.tag(("xtpl", b, tt), self.S_XTPL, eng.dma_start(
                out=bass.AP(AR, xtp, [[NCOLS, 128], [512, DC], [1, 512]]),
                in_=bass.AP(self.d_xtp, b * D * S + tt * 512,
                            [[S, 128], [128 * S, DC], [1, 512]])), 16)
            for fg in range(8):
                if fg >= 2:
                    self.wt(eng, ("f1", b, tt, fg - 2))
                elif prev is not None:
                    self.wt(eng, ("zmm", prev[0], prev[1], 6 + fg))
                for dmat, coff in ((self.d_mw1h, 0), (self.d_mw1l, 2048)):
                    for dc in range(DC):
                        ins = eng.dma_start(
                            out=bfv(MWB[fg % 2] + coff + dc * 256, 256),
                            in_=bass.AP(dmat, b * D * F + dc * 128 * F + fg * 512,
                                        [[F, 128], [1, 512]]))
                        self.tagn(self.S_MW1L[fg % 2], ins, 16)
                self.alias(("mw1l", b, tt, fg), self.S_MW1L[fg % 2])
            for hc in range(DC):
                if hc >= 2:
                    self.wt(eng, ("zmm", b, tt, hc - 2))
                else:
                    self.wt(eng, ("f1", b, tt, 6 + hc))
                for dmat, coff in ((self.d_mw2h, 0), (self.d_mw2l, 2048)):
                    for q in range(4):
                        ins = eng.dma_start(
                            out=bfv(MWB[hc % 2] + coff + q * 512, None,
                                    [[64, 8], [1, 64]]),
                            in_=bass.AP(dmat, ((b * DC + hc) * FC + q * 8) * 16384,
                                        [[128, 128], [16384, 8], [1, 128]]))
                        self.tagn(self.S_MW2L[hc % 2], ins, 16)
                self.alias(("mw2l", b, tt, hc), self.S_MW2L[hc % 2])

        def f_pe(eng):
            self.wt(eng, ("xsplit", b, tt))
            self.wt(eng, ("mbdone", b))
            for fg in range(8):
                self.wt(eng, ("mw1l", b, tt, fg))
                if fg >= 2:
                    for i in range(4):
                        self.wt(eng, ("gel", b, tt, (fg - 2) * 4 + i))
                if fg == 1 and prev is not None:
                    self.wt(eng, ("outr", prev[0], prev[1]))
                psd = self.PSA if fg % 2 == 0 else self.PSB
                mh = MWB[fg % 2]
                ml = MWB[fg % 2] + 2048
                _P3 = int(os.environ.get("P3", "3"))
                for dc in range(DC):
                    for i in range(4):
                        for pi, (mc, xc) in enumerate((((mh, xth), (mh, xtl), (ml, xth)))[:_P3]):
                            ins = eng.matmul(
                                bass.AP(psd, i * 512, [[2048, 128], [1, 512]]),
                                bfv(mc + dc * 256 + i * 64, 64),
                                bfv(xc + dc * 256, 256),
                                start=(dc == 0 and pi == 0),
                                stop=(dc == DC - 1 and pi == _P3 - 1))
                self.tag(("f1", b, tt, fg), self.MM, ins)
            for hc in range(DC):
                self.wt(eng, ("mw2l", b, tt, hc))
                if hc < 2:
                    self.wt(eng, ("gel", b, tt, 27))
                else:
                    self.wt(eng, ("z2", b, tt, hc - 2))
                mh = MWB[hc % 2]
                ml = MWB[hc % 2] + 2048
                _P3 = int(os.environ.get("P3", "3"))
                for fc in range(FC):
                    if fc % 4 == 0:
                        self.wt(eng, ("gel", b, tt, min(fc + 3, 31)))
                    for pi, (mc, uc) in enumerate((((mh, UHI), (mh, ULO), (ml, UHI)))[:_P3]):
                        ins = eng.matmul(
                            bass.AP(self.PSA, (hc % 2) * 512, [[2048, 128], [1, 512]]),
                            bfv(mc + fc * 64, 64),
                            bfv(uc + fc * 256, 256),
                            start=(fc == 0 and pi == 0),
                            stop=(fc == FC - 1 and pi == _P3 - 1))
                self.tag(("zmm", b, tt, hc), self.MM, ins)
            for hc in range(DC):
                self.wt(eng, ("z2", b, tt, hc))
                eng.matmul(self.psb(0, 1, 0, 512),
                           AR[0:128, C_ONES: C_ONES + 1],
                           AR[0:128, xtp + hc * 512: xtp + hc * 512 + 512],
                           start=(hc == 0), stop=(hc == DC - 1))
                self.wt(eng, ("fsq", b, tt, hc))
                ins2 = eng.matmul(self.psb(0, 1, 512, 512),
                                  AR[0:128, C_ONES: C_ONES + 1],
                                  AR[0:128, FSQ + (hc % 2) * 512: FSQ + (hc % 2) * 512 + 512],
                                  start=(hc == 0), stop=(hc == DC - 1))
                self.tag(("fssq", b, tt, hc), self.MM, ins2)
            self.wt(eng, ("frows", b, tt))
            eng.matmul(self.psb(0, 128, 1024, 512),
                       AR[0:1, C_ONES: C_ONES + 128],
                       AR[0:1, FR_MEAN: FR_MEAN + 512], start=True, stop=True)
            i2 = eng.matmul(self.psb(0, 128, 1536, 512),
                            AR[0:1, C_ONES: C_ONES + 128],
                            AR[0:1, FR_RSTD: FR_RSTD + 512], start=True, stop=True)
            self.tag(("fbc", b, tt), self.MM, i2)

        def f_act(eng):
            for fg in range(8):
                psd = self.PSA if fg % 2 == 0 else self.PSB
                self.wt(eng, ("f1", b, tt, fg))
                for i in range(4):
                    fc = fg * 4 + i
                    if fc >= 2:
                        self.wt(eng, ("gel", b, tt, fc - 2))
                    ins = eng.activation(
                        out=AR[0:128, UF32[fc % 2]: UF32[fc % 2] + 512],
                        in_=bass.AP(psd, i * 512, [[2048, 128], [1, 512]]),
                        func=AF.Gelu_apprx_tanh,
                        bias=AR[0:128, MB1 + 32 * b + fc: MB1 + 32 * b + fc + 1],
                        scale=1.0)
                    self.tag(("gelA", b, tt, fc), self.AC, ins)
            for hc in range(DC):
                self.wt(eng, ("z2", b, tt, hc))
                if hc >= 2:
                    self.wt(eng, ("fssq", b, tt, hc - 2))
                ins = eng.activation(
                    out=AR[0:128, FSQ + (hc % 2) * 512: FSQ + (hc % 2) * 512 + 512],
                    in_=AR[0:128, xtp + hc * 512: xtp + hc * 512 + 512],
                    func=AF.Square, bias=0.0, scale=1.0)
                self.tag(("fsq", b, tt, hc), self.AC, ins)
            self.wt(eng, ("fvar", b, tt))
            ins = eng.activation(out=AR[0:1, FR_T2: FR_T2 + 512],
                                 in_=AR[0:1, FR_T1: FR_T1 + 512],
                                 func=AF.Sqrt,
                                 bias=AR[0:1, C_EPS: C_EPS + 1], scale=1.0)
            self.tag(("fsd", b, tt), self.AC, ins)

        def f_vec(eng):
            self.wt(eng, ("xtpl", b, tt))
            if prev is not None:
                ppk = (b, tt - 2) if tt >= 2 else ((0, tt + 2) if b == 1 else None)
                if ppk is not None:
                    self.wt(eng, ("f1", ppk[0], ppk[1], 7))
            eng.tensor_copy(bfv(xth, 2048), bass.AP(AR, xtp, [[NCOLS, 128], [1, 4096]]))
            eng.tensor_tensor(out=bfv(xtl, 2048),
                              in0=bass.AP(AR, xtp, [[NCOLS, 128], [1, 4096]]),
                              in1=bfv(xth, 2048), op=ALU.subtract)
            self.tag(("xsplit", b, tt), self.DV, eng.drain())
            for fc in range(FC):
                self.wt(eng, ("gelA", b, tt, fc))
                eng.tensor_copy(bfv(UHI + fc * 256, 256),
                                AR[0:128, UF32[fc % 2]: UF32[fc % 2] + 512])
                eng.tensor_tensor(out=bfv(ULO + fc * 256, 256),
                                  in0=AR[0:128, UF32[fc % 2]: UF32[fc % 2] + 512],
                                  in1=bfv(UHI + fc * 256, 256), op=ALU.subtract)
                self.tag(("gel", b, tt, fc), self.DV, eng.drain())
                if b == 0:
                    self.mw_stt(eng, MWN + tt * FC + fc)
            for hc in range(DC):
                self.wt(eng, ("zmm", b, tt, hc))
                if prev is not None and hc == 0:
                    self.wt(eng, ("fsd", prev[0], prev[1]))
                ins = eng.scalar_tensor_tensor(
                    out=AR[0:128, xtp + hc * 512: xtp + hc * 512 + 512],
                    in0=bass.AP(self.PSA, (hc % 2) * 512, [[2048, 128], [1, 512]]),
                    scalar=AR[0:128, MB2 + 8 * b + hc: MB2 + 8 * b + hc + 1],
                    in1=AR[0:128, xtp + hc * 512: xtp + hc * 512 + 512],
                    op0=ALU.add, op1=ALU.add)
                self.tag(("z2", b, tt, hc), self.DV, ins)
            self.wt(eng, ("fssq", b, tt, DC - 1))
            eng.tensor_scalar_mul(AR[0:1, FR_MEAN: FR_MEAN + 512],
                                  self.psb(0, 1, 0, 512), 1.0 / 1024.0)
            eng.tensor_scalar_mul(AR[0:1, FR_T1: FR_T1 + 512],
                                  self.psb(0, 1, 512, 512), 1.0 / 1024.0)
            eng.tensor_tensor(out=AR[0:1, FR_T2: FR_T2 + 512],
                              in0=AR[0:1, FR_MEAN: FR_MEAN + 512],
                              in1=AR[0:1, FR_MEAN: FR_MEAN + 512], op=ALU.mult)
            ins = eng.tensor_tensor(out=AR[0:1, FR_T1: FR_T1 + 512],
                                    in0=AR[0:1, FR_T1: FR_T1 + 512],
                                    in1=AR[0:1, FR_T2: FR_T2 + 512], op=ALU.subtract)
            self.tag(("fvar", b, tt), self.DV, ins)
            self.wt(eng, ("fsd", b, tt))
            eng.reciprocal(AR[0:1, FR_RSTD: FR_RSTD + 512],
                           AR[0:1, FR_T2: FR_T2 + 512])
            self.tag(("frows", b, tt), self.DV, eng.drain())
            self.wt(eng, ("fbc", b, tt))
            for hc in range(DC):
                if hc >= 2:
                    self.wt(eng, ("outw", b, tt, hc - 2))
                elif prev is not None:
                    self.wt(eng, ("outw", prev[0], prev[1], 6 + hc))
                o2 = OUT2[hc % 2]
                eng.tensor_tensor(
                    out=AR[0:128, o2: o2 + 512],
                    in0=AR[0:128, xtp + hc * 512: xtp + hc * 512 + 512],
                    in1=self.psb(0, 128, 1024, 512), op=ALU.subtract)
                ins = eng.tensor_tensor(
                    out=AR[0:128, o2: o2 + 512],
                    in0=AR[0:128, o2: o2 + 512],
                    in1=self.psb(0, 128, 1536, 512), op=ALU.mult)
                self.tag(("outr", b, tt, hc), self.DV, ins)
            self.alias(("outr", b, tt), self.DV)

        def f_gps(eng):
            if b == 0:
                g0 = MWN + tt * FC
                self.mw_load(eng, g0)
                for ui in range(1, FC):
                    self.mw_load(eng, g0 + ui)
                    self.mw_write(eng, g0 + ui - 1)
                self.mw_write(eng, g0 + FC - 1)
            for hc in range(DC):
                self.wt(eng, ("outr", b, tt, hc))
                ins = eng.dma_start(
                    out=bass.AP(self.d_out, b * D * S + hc * 128 * S + tt * 512,
                                [[S, 128], [1, 512]]),
                    in_=bass.AP(AR, OUT2[hc % 2], [[NCOLS, 128], [1, 512]]))
                self.tag(("outw", b, tt, hc), self.SG, ins, 16)
            self.alias(("outw", b, tt), self.SG)

        self.emit_block([("sync", f_sync), ("tensor", f_pe), ("scalar", f_act),
                         ("vector", f_vec), ("gpsimd", f_gps)])

    def probe_dump(self):
        AR = self.AR

        def f_gps(eng):
            b = 0
            for p in range(NP if self.npairs is None else self.npairs):
                self.wt(eng, ("oshift", b, p))
                self.wt(eng, ("oev", b, p, 0, 15))
            cols = [(QT, 2048), (KT, 2048), (STAGE, 2048), (PT[0], 2048),
                    (PT[1], 2048), (VT, 2048), (STRIP, 1024), (NMC, 4), (VBR, 128)]
            off = 0
            for c0, w in cols:
                ins = eng.dma_start(
                    out=bass.AP(self.d_dbg, off, [[16384, 128], [1, w]]),
                    in_=AR[0:128, c0: c0 + w])
                self.tagn(self.SG, ins, 16)
                off += w
            ins = eng.dma_start(
                out=bass.AP(self.d_dbg, 128 * 16384, [[16384, 128], [1, 16384]]),
                in_=AR[0:128, BUFB: BUFB + 16384])
            self.tagn(self.SG, ins, 16)

        self.emit_block([("gpsimd", f_gps)])

    def gate_dump(self, slot=0):
        AR = self.AR

        def f_gps(eng):
            for b in range(B):
                self.wt(eng, ("mbdone", b))
                self.wt(eng, ("gbc", b))
            i1 = eng.dma_start(
                out=bass.AP(self.d_dbg, slot * 128 * 16384, [[16384, 128], [1, 1024]]),
                in_=AR[0:128, NCOLS - 1024: NCOLS])
            self.tagn(self.SG, i1, 16)
            if slot == 0 and self.stage == "fulldbg":
                self.wt(eng, ("outw", B - 1, TT - 1))
                i3 = eng.dma_start(
                    out=bass.AP(self.d_dbg, 4096, [[16384, 128], [1, 8192]]),
                    in_=AR[0:128, BUFA: BUFA + 8192])
                self.tagn(self.SG, i3, 16)
                i4 = eng.dma_start(
                    out=bass.AP(self.d_dbg, 12288, [[16384, 128], [1, 2048]]).bitcast(mybir.dt.bfloat16),
                    in_=bass.AP(self.d_mw1h, D * F, [[F, 128], [1, 4096]]))
                self.tagn(self.SG, i4, 16)

        self.emit_block([("gpsimd", f_gps)])

    def mw_wait(self):
        def f_sync(eng):
            if not self.mock:
                eng.wait_ge(self.SD.h, self.SD.n)
                eng.wait_ge(self.SG.h, self.SG.n)

        self.emit_block([("sync", f_sync)])

    def dbg_dump(self, b, what):
        AR = self.AR

        def f_gps(eng):
            if what == "attn":
                for pp in range(NP):
                    self.wt(eng, ("oev", b, pp, 0, 15))
                    self.wt(eng, ("oshift", b, pp))
            else:
                self.wt(eng, ("xtp", b, TT - 1))
            ins = eng.dma_start(
                out=bass.AP(self.d_dbg, b * 128 * 16384, [[16384, 128], [1, 16384]]),
                in_=AR[0:128, BUFB: BUFB + 16384])
            self.tag(("dbgw", b), self.SG, ins, 16)

        def f_sync(eng):
            if self.stage == "ln1" and b == 0:
                self.tag(("xt", 1), self.S_XT, eng.dma_start(
                    out=self.ar3(BUFA, [[2048, DC], [1, 2048]]),
                    in_=bass.AP(self.d_xt, D * S, [[S, 128], [128 * S, DC], [1, S]])), 16)

        self.emit_block([("gpsimd", f_gps), ("sync", f_sync)])

    def epilogue(self):
        def f_sync(eng):
            if not self.mock:
                eng.wait_ge(self.SG.h, self.SG.n)

        self.emit_block([("sync", f_sync)])


_CACHE = {}


def build_program(stage="full"):
    if stage not in _CACHE:
        _CACHE[stage] = Builder(stage).build()
    return _CACHE[stage]


def prepare_inputs(inputs):
    f = lambda k: np.ascontiguousarray(np.asarray(inputs[k], dtype=np.float32))
    x = f("L0_attn_x")
    qkv_w = f("L0_attn_qkv_w")
    qkv_b = f("L0_attn_qkv_b")
    proj_w = f("L0_attn_proj_w")
    proj_b = f("L0_attn_proj_b")
    gw1 = f("L0_ffn_gate_w1")
    gb1 = f("L0_ffn_gate_b1")
    gw2 = f("L0_ffn_gate_w2")
    gb2 = f("L0_ffn_gate_b2")
    W1 = f("L0_ffn_W1")
    b1 = f("L0_ffn_b1")
    W2 = f("L0_ffn_W2")
    b2 = f("L0_ffn_b2")
    TW1 = f("L0_ffn_TW1")
    Tb1 = f("L0_ffn_Tb1")
    TW2 = f("L0_ffn_TW2")
    Tb2 = f("L0_ffn_Tb2")

    qkvT = np.ascontiguousarray(qkv_w.T)
    wqkv = np.empty((NP, 128, 3072), np.float32)
    for p in range(NP):
        pb = np.concatenate([qkvT[:, 128 * p:128 * p + 128],
                             qkvT[:, 1024 + 128 * p:1024 + 128 * p + 128],
                             qkvT[:, 2048 + 128 * p:2048 + 128 * p + 128]], axis=1)
        wqkv[p] = pb.reshape(8, 128, 384).transpose(1, 0, 2).reshape(128, 3072)

    pwt = np.ascontiguousarray(proj_w.T.reshape(8, 128, 1024).transpose(1, 0, 2).reshape(128, 8192))
    gw1t = np.ascontiguousarray(gw1.T.reshape(8, 128, 1024).transpose(1, 0, 2).reshape(128, 8192))

    misc = np.zeros((128, MISC_W), np.float32)
    misc[:, C_ONES:C_ONES + 128] = 1.0
    misc[:, C_ID:C_ID + 128] = np.eye(128, dtype=np.float32)
    for p in range(NP):
        misc[:, C_BQKV + 3 * p + 0] = qkv_b[128 * p:128 * p + 128]
        misc[:, C_BQKV + 3 * p + 1] = qkv_b[1024 + 128 * p:1024 + 128 * p + 128]
        misc[:, C_BQKV + 3 * p + 2] = qkv_b[2048 + 128 * p:2048 + 128 * p + 128]
    misc[:, C_PJB:C_PJB + 8] = proj_b.reshape(8, 128).T
    misc[:, C_GB1:C_GB1 + 8] = gb1.reshape(8, 128).T
    misc[:, C_B1T:C_B1T + 32] = b1.reshape(32, 128).T
    misc[:, C_B2T:C_B2T + 8] = b2.reshape(8, 128).T
    misc[0:8, C_GB2] = gb2
    misc[:, C_EPS] = 1e-5
    misc[:, C_INV] = 1.0 / 2048.0
    misc[0, C_GB2R:C_GB2R + 8] = gb2
    misc[:, C_TB1:C_TB1 + 256] = np.ascontiguousarray(Tb1.T).reshape(32, 128, 8).transpose(1, 0, 2).reshape(128, 256)
    misc[:, C_TB2:C_TB2 + 64] = np.ascontiguousarray(Tb2.T).reshape(8, 128, 8).transpose(1, 0, 2).reshape(128, 64)
    misc[:, C_GW2:C_GW2 + 64] = np.ascontiguousarray(gw2.T).reshape(8, 128, 8).transpose(1, 0, 2).reshape(128, 64)

    w1t = np.ascontiguousarray(W1.T)
    tw1t = np.ascontiguousarray(TW1.transpose(0, 2, 1))
    w2t = np.ascontiguousarray(W2.T)
    tw2t = np.ascontiguousarray(TW2.transpose(0, 2, 1))

    xt_all = np.ascontiguousarray(x.transpose(0, 2, 1))

    shared = {"wqkv": wqkv, "pwt": pwt, "gw1t": gw1t, "misc": misc,
              "w1t": w1t, "tw1t": tw1t, "w2t": w2t, "tw2t": tw2t}
    in_maps = []
    for c in range(NCORE):
        m = dict(shared)
        m["xt"] = np.ascontiguousarray(xt_all[2 * c:2 * c + 2])
        in_maps.append(m)
    return in_maps


def kernel(**inputs):
    in_maps = prepare_inputs(inputs)
    nc = build_program("full")
    res = run_bass_kernel_spmd(nc, in_maps, core_ids=list(range(NCORE)))
    out = np.empty((16, 2048, 1024), np.float32)
    for c in range(NCORE):
        out[2 * c:2 * c + 2] = np.asarray(res.results[c]["outt"]).transpose(0, 2, 1)
    return out


# version 06
# speedup vs baseline: 1.0976x; 1.0976x over previous
import os
import numpy as np
import concourse.bass as bass
import concourse.mybir as mybir
from concourse.bass_utils import run_bass_kernel_spmd
from concourse.alu_op_type import AluOpType as ALU

F32 = mybir.dt.float32
AF = mybir.ActivationFunctionType
AX = mybir.AxisListType

NCORE = 8
B = 2
S = 2048
D = 1024
NP = 8
F = 4096
E = 8
DC = 8
FC = 32
TT = 4

NCOLS = 52736

C_ONES = 0
C_ID = 128
C_BQKV = 256
C_PJB = 280
C_GB1 = 288
C_B1T = 296
C_B2T = 328
C_GB2 = 336
C_TB1 = 337
C_TB2 = 593
C_GW2 = 657
C_EPS = 721
C_INV = 722
C_GB2R = 723
MISC_W = 1024

BUFA = 1024
BUFB = 17408
DYN = 33792

WB = 33792
STRIP = 36864
STRIPB = 37888
QT = 39936
KT = 41984
VT = 44032
PT = (46080, 48128)
STAGE = 50176
VBR = 52224
NMC = 52352

GBC = 52640
MB1 = 52656
MB2 = 52720

PWT = 33792
PZT = 41984
PSQ = 46176
PR_MEAN = 47200
PR_T1 = 47712
PR_T2 = 48224
PR_RSTD = 48736

GW1 = 33792
HT = 41984
GTR = 46166
TMP8 = 46176
GT2 = 46208
GT128 = 48384

MWB = (33792, 37888)
FSQ = 41984
FR_MEAN = 43008
FR_T1 = 43520
FR_T2 = 44032
FR_RSTD = 44544
OUT2 = (45056, 45568)
UF32 = (49152, 49664)

MW_B2 = BUFB + 7 * 2048
MW0 = dict(LW=(38912, 39168), LT=tuple(MW_B2 + 256 * k for k in range(6)),
           ACC=MW_B2 + 1536, ACCH=39424, ACCL=39552, w1w=256, w2w=256)
MW1 = dict(LW=(33792, 35840), LT=(37888, 39936, 41984),
           ACC=44032, ACCH=46080, ACCL=47104, w1w=2048, w2w=1024)
N_MW0 = DC * (F // 256) + FC * (D // 256)
N_MW1 = DC * (F // 2048) + FC * (D // 1024)
MW0_PP = (37, 37, 37, 37, 37, 37, 34, 0)
MW0_OFF = tuple(sum(MW0_PP[:p]) for p in range(NP))
XTP_ = (BUFB, BUFB + 4096)
XTH_ = (BUFB + 8192, BUFB + 10240)
XTL_ = (BUFB + 12288, BUFB + 14336)
UHI = BUFA
ULO = BUFA + 8192


class _MockInstr:
    def then_inc(self, h, k):
        return self


class _MockEng:
    def __getattr__(self, name):
        if name in ("wait_ge",):
            return lambda *a, **k: None
        return lambda *a, **k: _MockInstr()


_MOCK = _MockEng()


class Sem:
    def __init__(self, h):
        self.h = h
        self.n = 0
        self.e = 0


class Builder:
    def __init__(self, stage="full", npairs=None):
        self.stage = stage
        self.npairs = npairs
        nc = bass.Bass("TRN2", target_bir_lowering=False, debug=False)
        self.nc = nc
        self.T = {}
        self.mock = False

        self.d_xt = nc.dram_tensor("xt", [B, D, S], F32, kind="ExternalInput")
        self.d_wqkv = nc.dram_tensor("wqkv", [NP, 128, 3072], F32, kind="ExternalInput")
        self.d_pwt = nc.dram_tensor("pwt", [128, 8192], F32, kind="ExternalInput")
        self.d_gw1t = nc.dram_tensor("gw1t", [128, 8192], F32, kind="ExternalInput")
        self.d_misc = nc.dram_tensor("misc", [128, MISC_W], F32, kind="ExternalInput")
        self.d_w1t = nc.dram_tensor("w1t", [D, F], F32, kind="ExternalInput")
        self.d_tw1t = nc.dram_tensor("tw1t", [E, D, F], F32, kind="ExternalInput")
        self.d_w2t = nc.dram_tensor("w2t", [F, D], F32, kind="ExternalInput")
        self.d_tw2t = nc.dram_tensor("tw2t", [E, F, D], F32, kind="ExternalInput")
        self.d_out = nc.dram_tensor("outt", [B, D, S], F32, kind="ExternalOutput")
        mwkind = "ExternalOutput" if stage == "mw" else "Internal"
        BF16 = mybir.dt.bfloat16
        self.d_xtp = nc.dram_tensor("xtp_scr", [B, D, S], F32, kind=mwkind)
        self.d_mw1h = nc.dram_tensor("mw1h_scr", [B, D, F], BF16, kind=mwkind)
        self.d_mw1l = nc.dram_tensor("mw1l_scr", [B, D, F], BF16, kind=mwkind)
        self.d_mw2h = nc.dram_tensor("mw2h_scr", [B, DC, FC, 128, 128], BF16, kind=mwkind)
        self.d_mw2l = nc.dram_tensor("mw2l_scr", [B, DC, FC, 128, 128], BF16, kind=mwkind)
        if stage in ("attn", "ln1", "ln1x", "probe", "gate", "fulldbg"):
            self.d_dbg = nc.dram_tensor("dbg", [B, 128, 16384], F32, kind="ExternalOutput")

    def ar2(self, p0, np_, col, w):
        return bass.AP(self.AR, p0 * NCOLS + col, [[NCOLS, np_], [1, w]])

    def ar3(self, col, dims):
        return bass.AP(self.AR, col, [[NCOLS, 128]] + dims)

    def psa(self, p0, np_, col, w):
        return bass.AP(self.PSA, p0 * 2048 + col, [[2048, np_], [1, w]])

    def psb(self, p0, np_, col, w):
        return bass.AP(self.PSB, p0 * 2048 + col, [[2048, np_], [1, w]])

    def tag(self, key, sem, instr, k=1):
        if self.mock:
            sem.n += k
            assert key not in self.T, key
            self.T[key] = (sem, sem.n)
        else:
            sem.e += k
            assert self.T[key] == (sem, sem.e), (key, self.T[key], sem.e)
            instr.then_inc(sem.h, k)

    def tagn(self, sem, instr, k=1):
        if self.mock:
            sem.n += k
        else:
            sem.e += k
            instr.then_inc(sem.h, k)

    def alias(self, key2, sem):
        if self.mock:
            self.T[key2] = (sem, sem.n)

    def wt(self, eng, key):
        if self.mock:
            return
        sem, t = self.T[key]
        if id(sem) in self.dma_sems:
            sv = self.sealed.get(id(sem), 0)
            if sv >= t:
                t = sv
        eng.wait_ge(sem.h, t)

    def emit_block(self, sections):
        self.sealed = {id(sem): sem.n for sem in self.sems.values()}
        self.mock = True
        for _, fn in sections:
            fn(_MOCK)
        self.mock = False
        with self.nc.Block() as blk:
            for name, fn in sections:
                getattr(blk, name)(fn)

    def build(self):
        nc = self.nc
        import contextlib
        with contextlib.ExitStack() as ctx:
            AR = ctx.enter_context(nc.sbuf_tensor("AR", [128, NCOLS], F32))
            PSA = ctx.enter_context(nc.psum_tensor("PSA", [128, 2048], F32))
            PSB = ctx.enter_context(nc.psum_tensor("PSB", [128, 2048], F32))
            self.AR, self.PSA, self.PSB = AR, PSA, PSB
            names = ["SG", "SD", "MM", "AC", "DV", "MISC", "XT", "WBS", "PWT",
                     "GW1", "XTPL", "MW1L0", "MW1L1", "MW2L0", "MW2L1",
                     "LW0", "LW1", "LT0", "LT1", "LT2", "LT3", "LT4", "LT5"]
            sems = {n: Sem(ctx.enter_context(nc.semaphore(n))) for n in names}
            self.sems = sems
            self.SG, self.SD = sems["SG"], sems["SD"]
            self.MM, self.AC, self.DV = sems["MM"], sems["AC"], sems["DV"]
            self.S_MISC, self.S_XT, self.S_WB = sems["MISC"], sems["XT"], sems["WBS"]
            self.S_PWT, self.S_GW1, self.S_XTPL = sems["PWT"], sems["GW1"], sems["XTPL"]
            self.S_MW1L = (sems["MW1L0"], sems["MW1L1"])
            self.S_MW2L = (sems["MW2L0"], sems["MW2L1"])
            self.S_LW = (sems["LW0"], sems["LW1"])
            self.S_LT = tuple(sems["LT%d" % k] for k in range(6))
            self.dma_sems = {id(sems[n]) for n in names if n not in ("MM", "AC", "DV")}
            self.sealed = {}

            self.prologue()
            if self.stage == "ln1x":
                for p in range(NP):
                    self.a_pair(0, p)
                self.a_fin(0)
                for p in range(NP):
                    self.a_pair(1, p)
                self.a_fin(1)
                self.p_phase(1)
                self.dbg_dump(1, "ln1")
                self.epilogue()
                return nc
            if self.stage == "probe":
                for p in range(NP if self.npairs is None else self.npairs):
                    self.a_pair(0, p)
                self.probe_dump()
                self.epilogue()
                return nc
            for b in range(B):
                for p in range(NP):
                    self.a_pair(b, p)
                self.a_fin(b)
                if self.stage == "attn":
                    self.dbg_dump(b, "attn")
                    continue
                self.p_phase(b)
                if self.stage == "ln1":
                    self.dbg_dump(b, "ln1")
                    continue
                self.g_phase(b)
            if self.stage == "gate":
                self.gate_dump()
            if self.stage == "fulldbg":
                self.gate_dump(slot=1)
            if self.stage not in ("attn", "ln1", "gate"):
                if self.stage == "mw":
                    self.mw_standalone(0, N_MW0)
                    self.mw_standalone(N_MW0, N_MW0 + N_MW1)
                    self.mw_wait()
                    self.epilogue()
                    return nc
                self.mw_standalone(N_MW0, N_MW0 + N_MW1)
                prev = [None]
                for b in range(B):
                    for tt in range(TT):
                        self.f_block(b, tt, prev[0])
                        prev[0] = (b, tt)
                if self.stage == "fulldbg":
                    self.gate_dump()
            self.epilogue()
        return nc

    def prologue(self):
        def f_sync(eng):
            self.tag(("misc",), self.S_MISC, eng.dma_start(
                out=self.ar2(0, 128, 0, MISC_W),
                in_=bass.AP(self.d_misc, 0, [[MISC_W, 128], [1, MISC_W]])), 16)
            self.tag(("xt", 0), self.S_XT, eng.dma_start(
                out=self.ar3(BUFA, [[2048, DC], [1, 2048]]),
                in_=bass.AP(self.d_xt, 0, [[S, 128], [128 * S, DC], [1, S]])), 16)
            self.tag(("wb", 0, 0), self.S_WB, eng.dma_start(
                out=self.ar2(0, 128, WB, 3072),
                in_=bass.AP(self.d_wqkv, 0, [[3072, 128], [1, 3072]])), 16)

        self.emit_block([("sync", f_sync)])

    def a_pair(self, b, p):
        AR = self.AR
        inj = (b == 1 and self.stage in ("full", "fulldbg"))
        n_u, off_u = (MW0_PP[p], MW0_OFF[p]) if inj else (0, 0)

        def prev_it(h, it, back=1):
            g = h * 16 + it - back
            if g < 0:
                return None
            return (g // 16, g % 16)

        def f_sync(eng):
            if b > 0 and p == 0:
                self.tag(("wb", b, 0), self.S_WB, eng.dma_start(
                    out=self.ar2(0, 128, WB, 3072),
                    in_=bass.AP(self.d_wqkv, 0, [[3072, 128], [1, 3072]])), 16)
            if p + 1 < NP:
                self.wt(eng, ("vmm", b, p, 15))
                self.tag(("wb", b, p + 1), self.S_WB, eng.dma_start(
                    out=self.ar2(0, 128, WB, 3072),
                    in_=bass.AP(self.d_wqkv, (p + 1) * 128 * 3072,
                                [[3072, 128], [1, 3072]])), 16)

        def f_pe(eng):
            self.wt(eng, ("wb", b, p))
            self.wt(eng, ("misc",))
            self.wt(eng, ("xt", b))
            if ("exp", b, p - 1, 1, 15) in self.T:
                self.wt(eng, ("exp", b, p - 1, 1, 15))
                self.wt(eng, ("evB", b, p - 1, 1, 15))
                self.wt(eng, ("oev", b, p - 1, 1, 15))
            elif ("exp", b - 1, NP - 1, 1, 15) in self.T:
                self.wt(eng, ("exp", b - 1, NP - 1, 1, 15))
                self.wt(eng, ("evB", b - 1, NP - 1, 1, 15))
                self.wt(eng, ("oev", b - 1, NP - 1, 1, 15))

            tp = eng.transpose(self.psb(0, 1, 0, 128),
                               AR[0:128, C_BQKV + 3 * p + 2: C_BQKV + 3 * p + 3],
                               AR[0:128, C_ID: C_ID + 128])
            self.tag(("vbrow_mm", b, p), self.MM, tp)

            for role, colw, bank0 in (("q", 0, 0), ("k", 128, 1024)):
                for tt in range(TT):
                    if tt >= 2:
                        self.wt(eng, (role + "ev", b, p, tt - 2))
                    ps = self.psa(0, 128, bank0 + (tt % 2) * 512, 512)
                    for dc in range(DC):
                        ins = eng.matmul(
                            ps,
                            AR[0:128, WB + dc * 384 + colw: WB + dc * 384 + colw + 128],
                            AR[0:128, BUFA + dc * 2048 + tt * 512: BUFA + dc * 2048 + tt * 512 + 512],
                            start=(dc == 0), stop=(dc == DC - 1))
                    self.tag((role + "mm", b, p, tt), self.MM, ins)

            self.wt(eng, ("vbrow", b, p))
            for t2c in range(16):
                if t2c >= 4:
                    self.wt(eng, ("vev", b, p, t2c - 4))
                ps = self.psb(0, 128, (t2c % 4) * 512, 128)
                for dc in range(DC):
                    eng.matmul(
                        ps,
                        AR[0:128, BUFA + dc * 2048 + t2c * 128: BUFA + dc * 2048 + t2c * 128 + 128],
                        AR[0:128, WB + dc * 384 + 256: WB + dc * 384 + 384],
                        start=(dc == 0), stop=False)
                ins = eng.matmul(ps, AR[0:1, C_ONES: C_ONES + 128],
                                 AR[0:1, VBR: VBR + 128], start=False, stop=True)
                self.tag(("vmm", b, p, t2c), self.MM, ins)

            for h in range(2):
                base = h * 64
                for it in range(17):
                    k = it - 1
                    if it >= 1:
                        par = k % 2
                        self.wt(eng, ("norm", b, p, h, k))
                        if (h, k) == (0, 0):
                            self.wt(eng, ("vev", b, p, 15))
                        pv1b = prev_it(h, k, 1)
                        if pv1b is not None:
                            self.wt(eng, ("evB", b, p, pv1b[0], pv1b[1]))
                        for j in range(8):
                            ins = eng.matmul(
                                self.psb(0, 128, j * 128, 128),
                                AR[0:128, PT[par] + j * 128: PT[par] + j * 128 + 128],
                                AR[0:128, C_ID: C_ID + 128],
                                start=(j % 4 == 0), stop=(j % 4 == 3),
                                is_transpose=True)
                        self.tag(("tA", b, p, h, k), self.MM, ins)
                    if it < 16:
                        pv = prev_it(h, it)
                        if pv is None:
                            self.wt(eng, ("kev", b, p, TT - 1))
                            self.wt(eng, ("vev", b, p, 15))
                        else:
                            self.wt(eng, ("exp", b, p, pv[0], pv[1]))
                        for ns in range(4):
                            ins = eng.matmul(
                                self.psa(0, 128, ns * 512, 512),
                                AR[base: base + 64, QT + it * 128: QT + it * 128 + 128],
                                AR[base: base + 64, KT + ns * 512: KT + ns * 512 + 512],
                                start=True, stop=True)
                            self.tag(("smmp", b, p, h, it, ns), self.MM, ins)
                        self.alias(("smm", b, p, h, it), self.MM)
                    if it >= 1:
                        par = k % 2
                        self.wt(eng, ("evA", b, p, h, k))
                        for j in range(8, 16):
                            col = 1536 + (j - 8) * 128 if j < 12 else (j - 12) * 128
                            ins = eng.matmul(
                                self.psb(0, 128, col, 128),
                                AR[0:128, PT[par] + j * 128: PT[par] + j * 128 + 128],
                                AR[0:128, C_ID: C_ID + 128],
                                start=(j % 4 == 0), stop=(j % 4 == 3),
                                is_transpose=True)
                        self.tag(("tB", b, p, h, k), self.MM, ins)
                        pv1 = prev_it(h, k, 1)
                        if pv1 is not None:
                            self.wt(eng, ("oev", b, p, pv1[0], pv1[1]))
                        for t2c in range(8):
                            ins = eng.matmul(
                                bass.AP(self.PSB, 1024, [[2048, 64], [1, 128]]),
                                AR[0:128, VT + t2c * 128 + base: VT + t2c * 128 + base + 64],
                                AR[0:128, STRIP + t2c * 128: STRIP + t2c * 128 + 128],
                                start=(t2c == 0), stop=False)
                        self.wt(eng, ("evB", b, p, h, k))
                        for t2c in range(8, 16):
                            ins = eng.matmul(
                                bass.AP(self.PSB, 1024, [[2048, 64], [1, 128]]),
                                AR[0:128, VT + t2c * 128 + base: VT + t2c * 128 + base + 64],
                                AR[0:128, STRIPB + (t2c - 8) * 128: STRIPB + (t2c - 8) * 128 + 128],
                                start=False, stop=(t2c == 15))
                        self.tag(("pvB", b, p, h, k), self.MM, ins)

        def f_act(eng):
            for h in range(2):
                for it in range(17):
                    k = it - 1
                    if it >= 1:
                        self.wt(eng, ("tA", b, p, h, k))
                        ins = eng.activation(
                            out=AR[0:128, STRIP: STRIP + 1024],
                            in_=self.psb(0, 128, 0, 1024),
                            func=AF.Copy, bias=0.0, scale=1.0)
                        self.tag(("evA", b, p, h, k), self.AC, ins)
                        self.wt(eng, ("tB", b, p, h, k))
                        eng.activation(
                            out=AR[0:128, STRIPB: STRIPB + 512],
                            in_=self.psb(0, 128, 1536, 512),
                            func=AF.Copy, bias=0.0, scale=1.0)
                        ins = eng.activation(
                            out=AR[0:128, STRIPB + 512: STRIPB + 1024],
                            in_=self.psb(0, 128, 0, 512),
                            func=AF.Copy, bias=0.0, scale=1.0)
                        self.tag(("evB", b, p, h, k), self.AC, ins)
                    if it < 16:
                        self.wt(eng, ("smm", b, p, h, it))
                        self.wt(eng, ("bias", b, p, h, it))
                        ins = eng.activation(
                            out=AR[0:128, PT[it % 2]: PT[it % 2] + 2048],
                            in_=self.psa(0, 128, 0, 2048),
                            func=AF.Exp,
                            bias=AR[0:128, NMC + 1: NMC + 2],
                            scale=-1.0,
                            accum_out=AR[0:128, NMC + 2: NMC + 3])
                        self.tag(("exp", b, p, h, it), self.AC, ins)
                        self.tag(("sums", b, p, h, it), self.AC, eng.drain())

        def f_vec(eng):
            ninj = [0]

            def pace(itn):
                tgt = min(n_u, (itn * n_u + 33) // 34)
                while ninj[0] < tgt:
                    self.mw_stt(eng, off_u + ninj[0])
                    ninj[0] += 1

            self.wt(eng, ("vbrow_mm", b, p))
            ins = eng.tensor_copy(AR[0:1, VBR: VBR + 128], self.psb(0, 1, 0, 128))
            self.tag(("vbrow", b, p), self.DV, ins)
            for role, bank0, bcol in (("q", 0, 0), ("k", 1024, 1)):
                dstc = QT if role == "q" else KT
                for tt in range(TT):
                    self.wt(eng, (role + "mm", b, p, tt))
                    if role == "q":
                        ins = eng.tensor_scalar(
                            out=AR[0:128, dstc + tt * 512: dstc + tt * 512 + 512],
                            in0=self.psa(0, 128, bank0 + (tt % 2) * 512, 512),
                            scalar1=AR[0:128, C_BQKV + 3 * p + bcol: C_BQKV + 3 * p + bcol + 1],
                            scalar2=-0.125, op0=ALU.add, op1=ALU.mult)
                    else:
                        ins = eng.tensor_scalar_add(
                            AR[0:128, dstc + tt * 512: dstc + tt * 512 + 512],
                            self.psa(0, 128, bank0 + (tt % 2) * 512, 512),
                            AR[0:128, C_BQKV + 3 * p + bcol: C_BQKV + 3 * p + bcol + 1])
                    self.tag((role + "ev", b, p, tt), self.DV, ins)
            for t2c in range(16):
                self.wt(eng, ("vmm", b, p, t2c))
                ins = eng.tensor_copy(
                    AR[0:128, VT + t2c * 128: VT + t2c * 128 + 128],
                    self.psb(0, 128, (t2c % 4) * 512, 128))
                self.tag(("vev", b, p, t2c), self.DV, ins)
            for h in range(2):
                for it in range(17):
                    if it < 16:
                        for ns in range(4):
                            self.wt(eng, ("smmp", b, p, h, it, ns))
                            eng.tensor_reduce(
                                AR[0:128, NMC + 4 + ns: NMC + 5 + ns],
                                self.psa(0, 128, ns * 512, 512),
                                axis=AX.X, op=ALU.min)
                        eng.drain()
                        eng.tensor_reduce(
                            AR[0:128, NMC + 1: NMC + 2],
                            AR[0:128, NMC + 4: NMC + 8],
                            axis=AX.X, op=ALU.min)
                        self.tag(("bias", b, p, h, it), self.DV, eng.drain())
                        self.wt(eng, ("sums", b, p, h, it))
                        eng.reciprocal(AR[0:128, NMC + 3: NMC + 4],
                                       AR[0:128, NMC + 2: NMC + 3])
                        eng.drain()
                        ins = eng.tensor_scalar_mul(
                            AR[0:128, PT[it % 2]: PT[it % 2] + 2048],
                            AR[0:128, PT[it % 2]: PT[it % 2] + 2048],
                            AR[0:128, NMC + 3: NMC + 4])
                        self.tag(("norm", b, p, h, it), self.DV, ins)
                    if it >= 1:
                        k = it - 1
                        self.wt(eng, ("pvB", b, p, h, k))
                        if h == 1 and k == 0:
                            if ("oshift", b, p - 1) in self.T:
                                self.wt(eng, ("oshift", b, p - 1))
                            elif ("oshift", b - 1, NP - 1) in self.T:
                                self.wt(eng, ("oshift", b - 1, NP - 1))
                        if h == 0:
                            ins = eng.tensor_copy(
                                AR[0:64, BUFB + p * 2048 + k * 128: BUFB + p * 2048 + k * 128 + 128],
                                bass.AP(self.PSB, 1024, [[2048, 64], [1, 128]]))
                        else:
                            ins = eng.tensor_copy(
                                AR[0:64, STAGE + k * 128: STAGE + k * 128 + 128],
                                bass.AP(self.PSB, 1024, [[2048, 64], [1, 128]]))
                        self.tag(("oev", b, p, h, k), self.DV, ins)
                    pace(h * 17 + it + 1)

        def f_gps(eng):
            if n_u:
                self.mw_load(eng, off_u)
                for k in range(1, n_u):
                    self.mw_load(eng, off_u + k)
                    self.mw_write(eng, off_u + k - 1)
                self.mw_write(eng, off_u + n_u - 1)
            self.wt(eng, ("oev", b, p, 1, 15))
            ins = eng.dma_start(
                out=AR[64:128, BUFB + p * 2048: BUFB + p * 2048 + 2048],
                in_=AR[0:64, STAGE: STAGE + 2048])
            self.tag(("oshift", b, p), self.SG, ins, 16)

        self.emit_block([("sync", f_sync), ("tensor", f_pe), ("scalar", f_act),
                         ("vector", f_vec), ("gpsimd", f_gps)])

    def a_fin(self, b):
        AR = self.AR

        def f_sync(eng):
            if self.stage != "attn":
                self.tag(("pwt", b), self.S_PWT, eng.dma_start(
                    out=self.ar2(0, 128, PWT, 8192),
                    in_=bass.AP(self.d_pwt, 0, [[8192, 128], [1, 8192]])), 16)
                if self.stage == "ln1x" and b == 0:
                    self.tag(("xt", 1), self.S_XT, eng.dma_start(
                        out=self.ar3(BUFA, [[2048, DC], [1, 2048]]),
                        in_=bass.AP(self.d_xt, D * S, [[S, 128], [128 * S, DC], [1, S]])), 16)
            elif b == 0:
                self.tag(("xt", 1), self.S_XT, eng.dma_start(
                    out=self.ar3(BUFA, [[2048, DC], [1, 2048]]),
                    in_=bass.AP(self.d_xt, D * S, [[S, 128], [128 * S, DC], [1, S]])), 16)

        self.emit_block([("sync", f_sync)])

    def p_phase(self, b):
        AR = self.AR

        def f_pe(eng):
            self.wt(eng, ("pwt", b))
            for pp in range(NP):
                self.wt(eng, ("oev", b, pp, 0, 15))
                self.wt(eng, ("oshift", b, pp))
            for tt in range(TT):
                for dc in range(DC):
                    k = tt * DC + dc
                    if k >= 2:
                        self.wt(eng, ("zev", b, k - 2))
                    ps = self.psa(0, 128, (dc % 2) * 512, 512)
                    for pc in range(NP):
                        ins = eng.matmul(
                            ps,
                            AR[0:128, PWT + pc * 1024 + dc * 128: PWT + pc * 1024 + dc * 128 + 128],
                            AR[0:128, BUFB + pc * 2048 + tt * 512: BUFB + pc * 2048 + tt * 512 + 512],
                            start=(pc == 0), stop=(pc == NP - 1))
                    self.tag(("ymm", b, k), self.MM, ins)
                for dc in range(DC):
                    self.wt(eng, ("zev", b, tt * DC + dc))
                    eng.matmul(
                        self.psb(0, 1, 0, 512),
                        AR[0:128, C_ONES: C_ONES + 1],
                        AR[0:128, PZT + dc * 512: PZT + dc * 512 + 512],
                        start=(dc == 0), stop=(dc == DC - 1))
                    self.wt(eng, ("sq", b, tt, dc))
                    ins2 = eng.matmul(
                        self.psb(0, 1, 512, 512),
                        AR[0:128, C_ONES: C_ONES + 1],
                        AR[0:128, PSQ + (dc % 2) * 512: PSQ + (dc % 2) * 512 + 512],
                        start=(dc == 0), stop=(dc == DC - 1))
                    self.tag(("ssq", b, tt, dc), self.MM, ins2)
                self.wt(eng, ("rows", b, tt))
                eng.matmul(self.psb(0, 128, 1024, 512),
                           AR[0:1, C_ONES: C_ONES + 128],
                           AR[0:1, PR_MEAN: PR_MEAN + 512], start=True, stop=True)
                i2 = eng.matmul(self.psb(0, 128, 1536, 512),
                                AR[0:1, C_ONES: C_ONES + 128],
                                AR[0:1, PR_RSTD: PR_RSTD + 512], start=True, stop=True)
                self.tag(("bcln", b, tt), self.MM, i2)

        def f_vec(eng):
            for tt in range(TT):
                for dc in range(DC):
                    k = tt * DC + dc
                    self.wt(eng, ("ymm", b, k))
                    if tt > 0:
                        self.wt(eng, ("sq", b, tt - 1, dc))
                    ins = eng.scalar_tensor_tensor(
                        out=AR[0:128, PZT + dc * 512: PZT + dc * 512 + 512],
                        in0=self.psa(0, 128, (dc % 2) * 512, 512),
                        scalar=AR[0:128, C_PJB + dc: C_PJB + dc + 1],
                        in1=AR[0:128, BUFA + dc * 2048 + tt * 512: BUFA + dc * 2048 + tt * 512 + 512],
                        op0=ALU.add, op1=ALU.add)
                    self.tag(("zev", b, k), self.DV, ins)
                self.wt(eng, ("ssq", b, tt, DC - 1))
                eng.tensor_scalar_mul(AR[0:1, PR_MEAN: PR_MEAN + 512],
                                      self.psb(0, 1, 0, 512), 1.0 / 1024.0)
                eng.tensor_scalar_mul(AR[0:1, PR_T1: PR_T1 + 512],
                                      self.psb(0, 1, 512, 512), 1.0 / 1024.0)
                eng.tensor_tensor(out=AR[0:1, PR_T2: PR_T2 + 512],
                                  in0=AR[0:1, PR_MEAN: PR_MEAN + 512],
                                  in1=AR[0:1, PR_MEAN: PR_MEAN + 512], op=ALU.mult)
                ins = eng.tensor_tensor(out=AR[0:1, PR_T1: PR_T1 + 512],
                                        in0=AR[0:1, PR_T1: PR_T1 + 512],
                                        in1=AR[0:1, PR_T2: PR_T2 + 512], op=ALU.subtract)
                self.tag(("var", b, tt), self.DV, ins)
                self.wt(eng, ("sdrow", b, tt))
                eng.reciprocal(AR[0:1, PR_RSTD: PR_RSTD + 512],
                               AR[0:1, PR_T2: PR_T2 + 512])
                self.tag(("rows", b, tt), self.DV, eng.drain())
                self.wt(eng, ("bcln", b, tt))
                for dc in range(DC):
                    eng.tensor_tensor(
                        out=AR[0:128, BUFB + dc * 2048 + tt * 512: BUFB + dc * 2048 + tt * 512 + 512],
                        in0=AR[0:128, PZT + dc * 512: PZT + dc * 512 + 512],
                        in1=self.psb(0, 128, 1024, 512), op=ALU.subtract)
                    ins = eng.tensor_tensor(
                        out=AR[0:128, BUFB + dc * 2048 + tt * 512: BUFB + dc * 2048 + tt * 512 + 512],
                        in0=AR[0:128, BUFB + dc * 2048 + tt * 512: BUFB + dc * 2048 + tt * 512 + 512],
                        in1=self.psb(0, 128, 1536, 512), op=ALU.mult)
                self.tag(("xtp", b, tt), self.DV, ins)

        def f_act(eng):
            for tt in range(TT):
                for dc in range(DC):
                    self.wt(eng, ("zev", b, tt * DC + dc))
                    if tt * DC + dc >= 2:
                        self.wt(eng, ("ssq", b, tt, dc - 2) if dc >= 2
                                else ("ssq", b, tt - 1, DC + dc - 2))
                    ins = eng.activation(
                        out=AR[0:128, PSQ + (dc % 2) * 512: PSQ + (dc % 2) * 512 + 512],
                        in_=AR[0:128, PZT + dc * 512: PZT + dc * 512 + 512],
                        func=AF.Square, bias=0.0, scale=1.0)
                    self.tag(("sq", b, tt, dc), self.AC, ins)
                self.wt(eng, ("var", b, tt))
                ins = eng.activation(out=AR[0:1, PR_T2: PR_T2 + 512],
                                     in_=AR[0:1, PR_T1: PR_T1 + 512],
                                     func=AF.Sqrt,
                                     bias=AR[0:1, C_EPS: C_EPS + 1], scale=1.0)
                self.tag(("sdrow", b, tt), self.AC, ins)

        def f_gps(eng):
            for tt in range(TT):
                self.wt(eng, ("xtp", b, tt))
                ins = eng.dma_start(
                    out=bass.AP(self.d_xtp, b * D * S + tt * 512,
                                [[S, 128], [128 * S, DC], [1, 512]]),
                    in_=bass.AP(AR, BUFB + tt * 512, [[NCOLS, 128], [2048, DC], [1, 512]]))
                self.tag(("xtpw", b, tt), self.SG, ins, 16)

        self.emit_block([("tensor", f_pe), ("vector", f_vec), ("scalar", f_act),
                         ("gpsimd", f_gps)])

    def g_phase(self, b):
        AR = self.AR

        def f_sync(eng):
            self.tag(("gw1", b), self.S_GW1, eng.dma_start(
                out=self.ar2(0, 128, GW1, 8192),
                in_=bass.AP(self.d_gw1t, 0, [[8192, 128], [1, 8192]])), 16)
            if b == 0:
                self.tag(("xt", 1), self.S_XT, eng.dma_start(
                    out=self.ar3(BUFA, [[2048, DC], [1, 2048]]),
                    in_=bass.AP(self.d_xt, D * S, [[S, 128], [128 * S, DC], [1, S]])), 16)

        def f_pe(eng):
            self.wt(eng, ("gw1", b))
            for tt in range(TT):
                for oc in range(DC):
                    k = tt * DC + oc
                    if k >= 2:
                        self.wt(eng, ("hev", b, k - 2))
                    ps = self.psa(0, 128, (oc % 2) * 512, 512)
                    for dc in range(DC):
                        ins = eng.matmul(
                            ps,
                            AR[0:128, GW1 + dc * 1024 + oc * 128: GW1 + dc * 1024 + oc * 128 + 128],
                            AR[0:128, BUFB + dc * 2048 + tt * 512: BUFB + dc * 2048 + tt * 512 + 512],
                            start=(dc == 0), stop=(dc == DC - 1))
                    self.tag(("hmm", b, k), self.MM, ins)
                for oc in range(DC):
                    self.wt(eng, ("hev", b, tt * DC + oc))
                    ins = eng.matmul(
                        self.psb(0, 8, tt * 512, 512),
                        AR[0:128, C_GW2 + oc * 8: C_GW2 + oc * 8 + 8],
                        AR[0:128, HT + oc * 512: HT + oc * 512 + 512],
                        start=(oc == 0), stop=(oc == DC - 1))
                self.tag(("g2", b, tt), self.MM, ins)
            self.wt(eng, ("gevac", b))
            for c in range(16):
                ins = eng.matmul(self.psa(0, 128, 1024, 8),
                                 AR[0:8, GT2 + c * 128: GT2 + c * 128 + 128],
                                 AR[0:8, C_ID: C_ID + 8],
                                 start=(c == 0), stop=(c == 15), is_transpose=True)
            self.tag(("gtacc", b), self.MM, ins)
            self.wt(eng, ("g128", b))
            eng.matmul(self.psa(0, 1, 1536, 8),
                       AR[0:128, C_INV: C_INV + 1],
                       AR[0:128, GT128: GT128 + 8], start=True, stop=False)
            ins = eng.matmul(self.psa(0, 1, 1536, 8),
                             AR[0:1, C_ONES: C_ONES + 1],
                             AR[0:1, C_GB2R: C_GB2R + 8], start=False, stop=True)
            self.tag(("growmm", b), self.MM, ins)
            self.wt(eng, ("gtrow", b))
            ins = eng.matmul(self.psb(0, 128, 0, 8),
                             AR[0:1, C_ONES: C_ONES + 128],
                             AR[0:1, GTR: GTR + 8], start=True, stop=True)
            self.tag(("gbcmm", b), self.MM, ins)

        def f_act(eng):
            for tt in range(TT):
                for oc in range(DC):
                    k = tt * DC + oc
                    self.wt(eng, ("hmm", b, k))
                    ins = eng.activation(
                        out=AR[0:128, HT + oc * 512: HT + oc * 512 + 512],
                        in_=self.psa(0, 128, (oc % 2) * 512, 512),
                        func=AF.Relu, bias=AR[0:128, C_GB1 + oc: C_GB1 + oc + 1],
                        scale=1.0)
                    self.tag(("hev", b, k), self.AC, ins)
            for tt in range(TT):
                self.wt(eng, ("g2", b, tt))
            ins = eng.activation(out=AR[0:8, GT2: GT2 + 2048],
                                 in_=self.psb(0, 8, 0, 2048),
                                 func=AF.Copy, bias=0.0, scale=1.0)
            self.tag(("gevac", b), self.AC, ins)

        def f_vec(eng):
            self.wt(eng, ("gtacc", b))
            eng.tensor_copy(AR[0:128, GT128: GT128 + 8],
                            self.psa(0, 128, 1024, 8))
            self.tag(("g128", b), self.DV, eng.drain())
            self.wt(eng, ("growmm", b))
            eng.tensor_copy(AR[0:1, GTR: GTR + 8], self.psa(0, 1, 1536, 8))
            self.tag(("gtrow", b), self.DV, eng.drain())
            self.wt(eng, ("gbcmm", b))
            eng.tensor_copy(AR[0:128, GBC + 8 * b: GBC + 8 * b + 8],
                            self.psb(0, 128, 0, 8))
            self.tag(("gbc", b), self.DV, eng.drain())
            for nchunk, ctb, cbt, dst in ((32, C_TB1, C_B1T, MB1 + 32 * b),
                                          (8, C_TB2, C_B2T, MB2 + 8 * b)):
                for c in range(nchunk):
                    for e in range(E):
                        ins = eng.scalar_tensor_tensor(
                            out=AR[0:128, dst + c: dst + c + 1],
                            in0=AR[0:128, ctb + c * 8 + e: ctb + c * 8 + e + 1],
                            scalar=bass.AP(self.PSB, e, [[2048, 128], [1, 1]]),
                            in1=(AR[0:128, cbt + c: cbt + c + 1] if e == 0
                                 else AR[0:128, dst + c: dst + c + 1]),
                            op0=ALU.mult, op1=ALU.add)
            self.tag(("mbdone", b), self.DV, ins)

        self.emit_block([("sync", f_sync) if b == 0 else ("sync", f_sync),
                         ("tensor", f_pe), ("scalar", f_act), ("vector", f_vec)])

    def mw_cfg(self, gi):
        if gi < N_MW0:
            bb, L, ui = 0, MW0, gi
        else:
            bb, L, ui = 1, MW1, gi - N_MW0
        c1 = F // L["w1w"]
        if ui < DC * c1:
            kind, i, ch, w = "w1", ui // c1, ui % c1, L["w1w"]
        else:
            u2 = ui - DC * c1
            c2 = D // L["w2w"]
            kind, i, ch, w = "w2", u2 // c2, u2 % c2, L["w2w"]
        return bb, L, kind, i, ch, w

    def mw_load(self, eng, gi):
        bb, L, kind, i, ch, w = self.mw_cfg(gi)
        if kind == "w1":
            off, pitch = i * 128 * F + ch * w, F
            dw, dtw, tstep = self.d_w1t, self.d_tw1t, D * F
        else:
            off, pitch = i * 128 * D + ch * w, D
            dw, dtw, tstep = self.d_w2t, self.d_tw2t, F * D
        if gi >= 2:
            self.wt(eng, ("mst", gi - 2, 0))
        self.tag(("mlw", gi), self.S_LW[gi % 2], eng.dma_start(
            out=self.ar2(0, 128, L["LW"][gi % 2], w),
            in_=bass.AP(dw, off, [[pitch, 128], [1, w]])), 16)
        nlt = len(L["LT"])
        for e in range(E):
            if e >= nlt:
                self.wt(eng, ("mst", gi, e - nlt))
            elif gi >= 1:
                self.wt(eng, ("mst", gi - 1,
                              ((E - 1 - e % nlt) // nlt) * nlt + e % nlt))
            self.tag(("mlt", gi, e), self.S_LT[e % nlt], eng.dma_start(
                out=self.ar2(0, 128, L["LT"][e % nlt], w),
                in_=bass.AP(dtw, e * tstep + off, [[pitch, 128], [1, w]])), 16)

    def mw_stt(self, eng, gi):
        AR = self.AR
        bb, L, kind, i, ch, w = self.mw_cfg(gi)
        if gi in (0, N_MW0):
            self.wt(eng, ("gbc", bb))
        self.wt(eng, ("mlw", gi))
        nlt = len(L["LT"])
        for e in range(E):
            self.wt(eng, ("mlt", gi, e))
            ins = eng.scalar_tensor_tensor(
                out=AR[0:128, L["ACC"]: L["ACC"] + w],
                in0=AR[0:128, L["LT"][e % nlt]: L["LT"][e % nlt] + w],
                scalar=AR[0:128, GBC + 8 * bb + e: GBC + 8 * bb + e + 1],
                in1=(AR[0:128, L["LW"][gi % 2]: L["LW"][gi % 2] + w] if e == 0
                     else AR[0:128, L["ACC"]: L["ACC"] + w]),
                op0=ALU.mult, op1=ALU.add)
            self.tag(("mst", gi, e), self.DV, ins)
        if gi >= 1:
            self.wt(eng, ("mwr", gi - 1))
        BF16 = mybir.dt.bfloat16
        hiv = AR[0:128, L["ACCH"]: L["ACCH"] + w // 2].bitcast(BF16)
        lov = AR[0:128, L["ACCL"]: L["ACCL"] + w // 2].bitcast(BF16)
        eng.tensor_copy(hiv, AR[0:128, L["ACC"]: L["ACC"] + w])
        eng.tensor_tensor(out=lov, in0=AR[0:128, L["ACC"]: L["ACC"] + w],
                          in1=hiv, op=ALU.subtract)
        self.tagn(self.DV, eng.drain())
        self.alias(("macc", gi), self.DV)

    def mw_write(self, eng, gi):
        BF16 = mybir.dt.bfloat16
        bb, L, kind, i, ch, w = self.mw_cfg(gi)
        self.wt(eng, ("macc", gi))
        for dmat1, dmat2, reg in ((self.d_mw1h, self.d_mw2h, L["ACCH"]),
                                  (self.d_mw1l, self.d_mw2l, L["ACCL"])):
            if kind == "w1":
                ins = eng.dma_start(
                    out=bass.AP(dmat1, bb * D * F + i * 128 * F + ch * w,
                                [[F, 128], [1, w]]),
                    in_=bass.AP(self.AR, reg,
                                [[NCOLS, 128], [1, w // 2]]).bitcast(BF16))
            else:
                qn = w // 128
                ins = eng.dma_start(
                    out=bass.AP(dmat2, ((bb * DC + ch * qn) * FC + i) * 16384,
                                [[128, 128], [FC * 16384, qn], [1, 128]]),
                    in_=bass.AP(self.AR, reg,
                                [[NCOLS, 128], [64, qn], [1, 64]]).bitcast(BF16))
            self.tagn(self.SD, ins, 16)
        self.alias(("mwr", gi), self.SD)
        if gi == N_MW0 - 1:
            self.alias(("mwall", 0), self.SD)
        if gi == N_MW0 + N_MW1 - 1:
            self.alias(("mwall", 1), self.SD)

    def mw_standalone(self, lo, hi):
        def f_sync(eng):
            for gi in range(lo, hi):
                self.mw_load(eng, gi)

        def f_vec(eng):
            for gi in range(lo, hi):
                self.mw_stt(eng, gi)

        def f_act(eng):
            for gi in range(lo, hi):
                self.mw_write(eng, gi)

        self.emit_block([("sync", f_sync), ("vector", f_vec), ("scalar", f_act)])

    def f_block(self, b, tt, prev):
        AR = self.AR
        BF16 = mybir.dt.bfloat16
        xtp = XTP_[tt % 2]
        xth = XTH_[tt % 2]
        xtl = XTL_[tt % 2]

        def bfv(col, w32, dims=None):
            if dims is None:
                return bass.AP(AR, col, [[NCOLS, 128], [1, w32]]).bitcast(BF16)
            return bass.AP(AR, col, [[NCOLS, 128]] + dims).bitcast(BF16)

        def f_sync(eng):
            if tt == 0:
                self.wt(eng, ("mwall", b))
            self.wt(eng, ("xtpw", b, tt))
            pk = None
            if tt >= 2:
                pk = (b, tt - 2)
            elif b == 1:
                pk = (0, tt + 2)
            if pk is not None:
                self.wt(eng, ("outr", pk[0], pk[1]))
                self.wt(eng, ("fsd", pk[0], pk[1]))
                self.wt(eng, ("f1", pk[0], pk[1], 7))
            self.tag(("xtpl", b, tt), self.S_XTPL, eng.dma_start(
                out=bass.AP(AR, xtp, [[NCOLS, 128], [512, DC], [1, 512]]),
                in_=bass.AP(self.d_xtp, b * D * S + tt * 512,
                            [[S, 128], [128 * S, DC], [1, 512]])), 16)
            for fg in range(8):
                if fg >= 2:
                    self.wt(eng, ("f1", b, tt, fg - 2))
                elif prev is not None:
                    self.wt(eng, ("zmm", prev[0], prev[1], 6 + fg))
                for dmat, coff in ((self.d_mw1h, 0), (self.d_mw1l, 2048)):
                    for dc in range(DC):
                        ins = eng.dma_start(
                            out=bfv(MWB[fg % 2] + coff + dc * 256, 256),
                            in_=bass.AP(dmat, b * D * F + dc * 128 * F + fg * 512,
                                        [[F, 128], [1, 512]]))
                        self.tagn(self.S_MW1L[fg % 2], ins, 16)
                self.alias(("mw1l", b, tt, fg), self.S_MW1L[fg % 2])
            for hc in range(DC):
                if hc >= 2:
                    self.wt(eng, ("zmm", b, tt, hc - 2))
                else:
                    self.wt(eng, ("f1", b, tt, 6 + hc))
                for dmat, coff in ((self.d_mw2h, 0), (self.d_mw2l, 2048)):
                    for q in range(4):
                        ins = eng.dma_start(
                            out=bfv(MWB[hc % 2] + coff + q * 512, None,
                                    [[64, 8], [1, 64]]),
                            in_=bass.AP(dmat, ((b * DC + hc) * FC + q * 8) * 16384,
                                        [[128, 128], [16384, 8], [1, 128]]))
                        self.tagn(self.S_MW2L[hc % 2], ins, 16)
                self.alias(("mw2l", b, tt, hc), self.S_MW2L[hc % 2])

        def f_pe(eng):
            self.wt(eng, ("xsplit", b, tt))
            self.wt(eng, ("mbdone", b))
            for fg in range(8):
                self.wt(eng, ("mw1l", b, tt, fg))
                if fg >= 2:
                    for i in range(4):
                        self.wt(eng, ("gel", b, tt, (fg - 2) * 4 + i))
                if fg == 1 and prev is not None:
                    self.wt(eng, ("outr", prev[0], prev[1]))
                psd = self.PSA if fg % 2 == 0 else self.PSB
                mh = MWB[fg % 2]
                ml = MWB[fg % 2] + 2048
                _P3 = int(os.environ.get("P3", "3"))
                for dc in range(DC):
                    for i in range(4):
                        for pi, (mc, xc) in enumerate((((mh, xth), (mh, xtl), (ml, xth)))[:_P3]):
                            ins = eng.matmul(
                                bass.AP(psd, i * 512, [[2048, 128], [1, 512]]),
                                bfv(mc + dc * 256 + i * 64, 64),
                                bfv(xc + dc * 256, 256),
                                start=(dc == 0 and pi == 0),
                                stop=(dc == DC - 1 and pi == _P3 - 1))
                self.tag(("f1", b, tt, fg), self.MM, ins)
            for hc in range(DC):
                self.wt(eng, ("mw2l", b, tt, hc))
                if hc < 2:
                    self.wt(eng, ("gel", b, tt, 27))
                else:
                    self.wt(eng, ("z2", b, tt, hc - 2))
                mh = MWB[hc % 2]
                ml = MWB[hc % 2] + 2048
                _P3 = int(os.environ.get("P3", "3"))
                for fc in range(FC):
                    if fc % 4 == 0:
                        self.wt(eng, ("gel", b, tt, min(fc + 3, 31)))
                    for pi, (mc, uc) in enumerate((((mh, UHI), (mh, ULO), (ml, UHI)))[:_P3]):
                        ins = eng.matmul(
                            bass.AP(self.PSA, (hc % 2) * 512, [[2048, 128], [1, 512]]),
                            bfv(mc + fc * 64, 64),
                            bfv(uc + fc * 256, 256),
                            start=(fc == 0 and pi == 0),
                            stop=(fc == FC - 1 and pi == _P3 - 1))
                self.tag(("zmm", b, tt, hc), self.MM, ins)
            for hc in range(DC):
                self.wt(eng, ("z2", b, tt, hc))
                eng.matmul(self.psb(0, 1, 0, 512),
                           AR[0:128, C_ONES: C_ONES + 1],
                           AR[0:128, xtp + hc * 512: xtp + hc * 512 + 512],
                           start=(hc == 0), stop=(hc == DC - 1))
                self.wt(eng, ("fsq", b, tt, hc))
                ins2 = eng.matmul(self.psb(0, 1, 512, 512),
                                  AR[0:128, C_ONES: C_ONES + 1],
                                  AR[0:128, FSQ + (hc % 2) * 512: FSQ + (hc % 2) * 512 + 512],
                                  start=(hc == 0), stop=(hc == DC - 1))
                self.tag(("fssq", b, tt, hc), self.MM, ins2)
            self.wt(eng, ("frows", b, tt))
            eng.matmul(self.psb(0, 128, 1024, 512),
                       AR[0:1, C_ONES: C_ONES + 128],
                       AR[0:1, FR_MEAN: FR_MEAN + 512], start=True, stop=True)
            i2 = eng.matmul(self.psb(0, 128, 1536, 512),
                            AR[0:1, C_ONES: C_ONES + 128],
                            AR[0:1, FR_RSTD: FR_RSTD + 512], start=True, stop=True)
            self.tag(("fbc", b, tt), self.MM, i2)

        def f_act(eng):
            for fg in range(8):
                psd = self.PSA if fg % 2 == 0 else self.PSB
                self.wt(eng, ("f1", b, tt, fg))
                for i in range(4):
                    fc = fg * 4 + i
                    if fc >= 2:
                        self.wt(eng, ("gel", b, tt, fc - 2))
                    ins = eng.activation(
                        out=AR[0:128, UF32[fc % 2]: UF32[fc % 2] + 512],
                        in_=bass.AP(psd, i * 512, [[2048, 128], [1, 512]]),
                        func=AF.Gelu_apprx_tanh,
                        bias=AR[0:128, MB1 + 32 * b + fc: MB1 + 32 * b + fc + 1],
                        scale=1.0)
                    self.tag(("gelA", b, tt, fc), self.AC, ins)
            for hc in range(DC):
                self.wt(eng, ("z2", b, tt, hc))
                if hc >= 2:
                    self.wt(eng, ("fssq", b, tt, hc - 2))
                ins = eng.activation(
                    out=AR[0:128, FSQ + (hc % 2) * 512: FSQ + (hc % 2) * 512 + 512],
                    in_=AR[0:128, xtp + hc * 512: xtp + hc * 512 + 512],
                    func=AF.Square, bias=0.0, scale=1.0)
                self.tag(("fsq", b, tt, hc), self.AC, ins)
            self.wt(eng, ("fvar", b, tt))
            ins = eng.activation(out=AR[0:1, FR_T2: FR_T2 + 512],
                                 in_=AR[0:1, FR_T1: FR_T1 + 512],
                                 func=AF.Sqrt,
                                 bias=AR[0:1, C_EPS: C_EPS + 1], scale=1.0)
            self.tag(("fsd", b, tt), self.AC, ins)

        def f_vec(eng):
            self.wt(eng, ("xtpl", b, tt))
            if prev is not None:
                ppk = (b, tt - 2) if tt >= 2 else ((0, tt + 2) if b == 1 else None)
                if ppk is not None:
                    self.wt(eng, ("f1", ppk[0], ppk[1], 7))
            eng.tensor_copy(bfv(xth, 2048), bass.AP(AR, xtp, [[NCOLS, 128], [1, 4096]]))
            eng.tensor_tensor(out=bfv(xtl, 2048),
                              in0=bass.AP(AR, xtp, [[NCOLS, 128], [1, 4096]]),
                              in1=bfv(xth, 2048), op=ALU.subtract)
            self.tag(("xsplit", b, tt), self.DV, eng.drain())
            for fc in range(FC):
                self.wt(eng, ("gelA", b, tt, fc))
                eng.tensor_copy(bfv(UHI + fc * 256, 256),
                                AR[0:128, UF32[fc % 2]: UF32[fc % 2] + 512])
                eng.tensor_tensor(out=bfv(ULO + fc * 256, 256),
                                  in0=AR[0:128, UF32[fc % 2]: UF32[fc % 2] + 512],
                                  in1=bfv(UHI + fc * 256, 256), op=ALU.subtract)
                self.tag(("gel", b, tt, fc), self.DV, eng.drain())
            for hc in range(DC):
                self.wt(eng, ("zmm", b, tt, hc))
                if prev is not None and hc == 0:
                    self.wt(eng, ("fsd", prev[0], prev[1]))
                ins = eng.scalar_tensor_tensor(
                    out=AR[0:128, xtp + hc * 512: xtp + hc * 512 + 512],
                    in0=bass.AP(self.PSA, (hc % 2) * 512, [[2048, 128], [1, 512]]),
                    scalar=AR[0:128, MB2 + 8 * b + hc: MB2 + 8 * b + hc + 1],
                    in1=AR[0:128, xtp + hc * 512: xtp + hc * 512 + 512],
                    op0=ALU.add, op1=ALU.add)
                self.tag(("z2", b, tt, hc), self.DV, ins)
            self.wt(eng, ("fssq", b, tt, DC - 1))
            eng.tensor_scalar_mul(AR[0:1, FR_MEAN: FR_MEAN + 512],
                                  self.psb(0, 1, 0, 512), 1.0 / 1024.0)
            eng.tensor_scalar_mul(AR[0:1, FR_T1: FR_T1 + 512],
                                  self.psb(0, 1, 512, 512), 1.0 / 1024.0)
            eng.tensor_tensor(out=AR[0:1, FR_T2: FR_T2 + 512],
                              in0=AR[0:1, FR_MEAN: FR_MEAN + 512],
                              in1=AR[0:1, FR_MEAN: FR_MEAN + 512], op=ALU.mult)
            ins = eng.tensor_tensor(out=AR[0:1, FR_T1: FR_T1 + 512],
                                    in0=AR[0:1, FR_T1: FR_T1 + 512],
                                    in1=AR[0:1, FR_T2: FR_T2 + 512], op=ALU.subtract)
            self.tag(("fvar", b, tt), self.DV, ins)
            self.wt(eng, ("fsd", b, tt))
            eng.reciprocal(AR[0:1, FR_RSTD: FR_RSTD + 512],
                           AR[0:1, FR_T2: FR_T2 + 512])
            self.tag(("frows", b, tt), self.DV, eng.drain())
            self.wt(eng, ("fbc", b, tt))
            for hc in range(DC):
                if hc >= 2:
                    self.wt(eng, ("outw", b, tt, hc - 2))
                elif prev is not None:
                    self.wt(eng, ("outw", prev[0], prev[1], 6 + hc))
                o2 = OUT2[hc % 2]
                eng.tensor_tensor(
                    out=AR[0:128, o2: o2 + 512],
                    in0=AR[0:128, xtp + hc * 512: xtp + hc * 512 + 512],
                    in1=self.psb(0, 128, 1024, 512), op=ALU.subtract)
                ins = eng.tensor_tensor(
                    out=AR[0:128, o2: o2 + 512],
                    in0=AR[0:128, o2: o2 + 512],
                    in1=self.psb(0, 128, 1536, 512), op=ALU.mult)
                self.tag(("outr", b, tt, hc), self.DV, ins)
            self.alias(("outr", b, tt), self.DV)

        def f_gps(eng):
            for hc in range(DC):
                self.wt(eng, ("outr", b, tt, hc))
                ins = eng.dma_start(
                    out=bass.AP(self.d_out, b * D * S + hc * 128 * S + tt * 512,
                                [[S, 128], [1, 512]]),
                    in_=bass.AP(AR, OUT2[hc % 2], [[NCOLS, 128], [1, 512]]))
                self.tag(("outw", b, tt, hc), self.SG, ins, 16)
            self.alias(("outw", b, tt), self.SG)

        self.emit_block([("sync", f_sync), ("tensor", f_pe), ("scalar", f_act),
                         ("vector", f_vec), ("gpsimd", f_gps)])

    def probe_dump(self):
        AR = self.AR

        def f_gps(eng):
            b = 0
            for p in range(NP if self.npairs is None else self.npairs):
                self.wt(eng, ("oshift", b, p))
                self.wt(eng, ("oev", b, p, 0, 15))
            cols = [(QT, 2048), (KT, 2048), (STAGE, 2048), (PT[0], 2048),
                    (PT[1], 2048), (VT, 2048), (STRIP, 1024), (NMC, 4), (VBR, 128)]
            off = 0
            for c0, w in cols:
                ins = eng.dma_start(
                    out=bass.AP(self.d_dbg, off, [[16384, 128], [1, w]]),
                    in_=AR[0:128, c0: c0 + w])
                self.tagn(self.SG, ins, 16)
                off += w
            ins = eng.dma_start(
                out=bass.AP(self.d_dbg, 128 * 16384, [[16384, 128], [1, 16384]]),
                in_=AR[0:128, BUFB: BUFB + 16384])
            self.tagn(self.SG, ins, 16)

        self.emit_block([("gpsimd", f_gps)])

    def gate_dump(self, slot=0):
        AR = self.AR

        def f_gps(eng):
            for b in range(B):
                self.wt(eng, ("mbdone", b))
                self.wt(eng, ("gbc", b))
            i1 = eng.dma_start(
                out=bass.AP(self.d_dbg, slot * 128 * 16384, [[16384, 128], [1, 1024]]),
                in_=AR[0:128, NCOLS - 1024: NCOLS])
            self.tagn(self.SG, i1, 16)
            if slot == 0 and self.stage == "fulldbg":
                self.wt(eng, ("outw", B - 1, TT - 1))
                i3 = eng.dma_start(
                    out=bass.AP(self.d_dbg, 4096, [[16384, 128], [1, 8192]]),
                    in_=AR[0:128, BUFA: BUFA + 8192])
                self.tagn(self.SG, i3, 16)
                i4 = eng.dma_start(
                    out=bass.AP(self.d_dbg, 12288, [[16384, 128], [1, 2048]]).bitcast(mybir.dt.bfloat16),
                    in_=bass.AP(self.d_mw1h, D * F, [[F, 128], [1, 4096]]))
                self.tagn(self.SG, i4, 16)

        self.emit_block([("gpsimd", f_gps)])

    def mw_wait(self):
        def f_sync(eng):
            if not self.mock:
                eng.wait_ge(self.SD.h, self.SD.n)
                eng.wait_ge(self.SG.h, self.SG.n)

        self.emit_block([("sync", f_sync)])

    def dbg_dump(self, b, what):
        AR = self.AR

        def f_gps(eng):
            if what == "attn":
                for pp in range(NP):
                    self.wt(eng, ("oev", b, pp, 0, 15))
                    self.wt(eng, ("oshift", b, pp))
            else:
                self.wt(eng, ("xtp", b, TT - 1))
            ins = eng.dma_start(
                out=bass.AP(self.d_dbg, b * 128 * 16384, [[16384, 128], [1, 16384]]),
                in_=AR[0:128, BUFB: BUFB + 16384])
            self.tag(("dbgw", b), self.SG, ins, 16)

        def f_sync(eng):
            if self.stage == "ln1" and b == 0:
                self.tag(("xt", 1), self.S_XT, eng.dma_start(
                    out=self.ar3(BUFA, [[2048, DC], [1, 2048]]),
                    in_=bass.AP(self.d_xt, D * S, [[S, 128], [128 * S, DC], [1, S]])), 16)

        self.emit_block([("gpsimd", f_gps), ("sync", f_sync)])

    def epilogue(self):
        def f_sync(eng):
            if not self.mock:
                eng.wait_ge(self.SG.h, self.SG.n)

        self.emit_block([("sync", f_sync)])


_CACHE = {}


def build_program(stage="full"):
    if stage not in _CACHE:
        _CACHE[stage] = Builder(stage).build()
    return _CACHE[stage]


def prepare_inputs(inputs):
    f = lambda k: np.ascontiguousarray(np.asarray(inputs[k], dtype=np.float32))
    x = f("L0_attn_x")
    qkv_w = f("L0_attn_qkv_w")
    qkv_b = f("L0_attn_qkv_b")
    proj_w = f("L0_attn_proj_w")
    proj_b = f("L0_attn_proj_b")
    gw1 = f("L0_ffn_gate_w1")
    gb1 = f("L0_ffn_gate_b1")
    gw2 = f("L0_ffn_gate_w2")
    gb2 = f("L0_ffn_gate_b2")
    W1 = f("L0_ffn_W1")
    b1 = f("L0_ffn_b1")
    W2 = f("L0_ffn_W2")
    b2 = f("L0_ffn_b2")
    TW1 = f("L0_ffn_TW1")
    Tb1 = f("L0_ffn_Tb1")
    TW2 = f("L0_ffn_TW2")
    Tb2 = f("L0_ffn_Tb2")

    qkvT = np.ascontiguousarray(qkv_w.T)
    wqkv = np.empty((NP, 128, 3072), np.float32)
    for p in range(NP):
        pb = np.concatenate([qkvT[:, 128 * p:128 * p + 128],
                             qkvT[:, 1024 + 128 * p:1024 + 128 * p + 128],
                             qkvT[:, 2048 + 128 * p:2048 + 128 * p + 128]], axis=1)
        wqkv[p] = pb.reshape(8, 128, 384).transpose(1, 0, 2).reshape(128, 3072)

    pwt = np.ascontiguousarray(proj_w.T.reshape(8, 128, 1024).transpose(1, 0, 2).reshape(128, 8192))
    gw1t = np.ascontiguousarray(gw1.T.reshape(8, 128, 1024).transpose(1, 0, 2).reshape(128, 8192))

    misc = np.zeros((128, MISC_W), np.float32)
    misc[:, C_ONES:C_ONES + 128] = 1.0
    misc[:, C_ID:C_ID + 128] = np.eye(128, dtype=np.float32)
    for p in range(NP):
        misc[:, C_BQKV + 3 * p + 0] = qkv_b[128 * p:128 * p + 128]
        misc[:, C_BQKV + 3 * p + 1] = qkv_b[1024 + 128 * p:1024 + 128 * p + 128]
        misc[:, C_BQKV + 3 * p + 2] = qkv_b[2048 + 128 * p:2048 + 128 * p + 128]
    misc[:, C_PJB:C_PJB + 8] = proj_b.reshape(8, 128).T
    misc[:, C_GB1:C_GB1 + 8] = gb1.reshape(8, 128).T
    misc[:, C_B1T:C_B1T + 32] = b1.reshape(32, 128).T
    misc[:, C_B2T:C_B2T + 8] = b2.reshape(8, 128).T
    misc[0:8, C_GB2] = gb2
    misc[:, C_EPS] = 1e-5
    misc[:, C_INV] = 1.0 / 2048.0
    misc[0, C_GB2R:C_GB2R + 8] = gb2
    misc[:, C_TB1:C_TB1 + 256] = np.ascontiguousarray(Tb1.T).reshape(32, 128, 8).transpose(1, 0, 2).reshape(128, 256)
    misc[:, C_TB2:C_TB2 + 64] = np.ascontiguousarray(Tb2.T).reshape(8, 128, 8).transpose(1, 0, 2).reshape(128, 64)
    misc[:, C_GW2:C_GW2 + 64] = np.ascontiguousarray(gw2.T).reshape(8, 128, 8).transpose(1, 0, 2).reshape(128, 64)

    w1t = np.ascontiguousarray(W1.T)
    tw1t = np.ascontiguousarray(TW1.transpose(0, 2, 1))
    w2t = np.ascontiguousarray(W2.T)
    tw2t = np.ascontiguousarray(TW2.transpose(0, 2, 1))

    xt_all = np.ascontiguousarray(x.transpose(0, 2, 1))

    shared = {"wqkv": wqkv, "pwt": pwt, "gw1t": gw1t, "misc": misc,
              "w1t": w1t, "tw1t": tw1t, "w2t": w2t, "tw2t": tw2t}
    in_maps = []
    for c in range(NCORE):
        m = dict(shared)
        m["xt"] = np.ascontiguousarray(xt_all[2 * c:2 * c + 2])
        in_maps.append(m)
    return in_maps


def kernel(**inputs):
    in_maps = prepare_inputs(inputs)
    nc = build_program("full")
    res = run_bass_kernel_spmd(nc, in_maps, core_ids=list(range(NCORE)))
    out = np.empty((16, 2048, 1024), np.float32)
    for c in range(NCORE):
        out[2 * c:2 * c + 2] = np.asarray(res.results[c]["outt"]).transpose(0, 2, 1)
    return out


# version 07
# speedup vs baseline: 1.1632x; 1.0598x over previous
import os
import numpy as np
import concourse.bass as bass
import concourse.mybir as mybir
from concourse.bass_utils import run_bass_kernel_spmd
from concourse.alu_op_type import AluOpType as ALU

F32 = mybir.dt.float32
AF = mybir.ActivationFunctionType
AX = mybir.AxisListType

NCORE = 8
B = 2
S = 2048
D = 1024
NP = 8
F = 4096
E = 8
DC = 8
FC = 32
TT = 4

NCOLS = 52736

C_ONES = 0
C_ID = 128
C_BQKV = 256
C_PJB = 280
C_GB1 = 288
C_B1T = 296
C_B2T = 328
C_GB2 = 336
C_TB1 = 337
C_TB2 = 593
C_GW2 = 657
C_EPS = 721
C_INV = 722
C_GB2R = 723
MISC_W = 1024

BUFA = 1024
BUFB = 17408
DYN = 33792

WB = 33792
STRIP = 36864
STRIPB = 37888
QT = 39936
KT = 41984
VT = 44032
PT = (46080, 48128)
STAGE = 50176
VBR = 52224
NMC = 52352

GBC = 52640
MB1 = 52656
MB2 = 52720

PWT = 33792
PZT = 41984
PSQ = 46176
PR_MEAN = 47200
PR_T1 = 47712
PR_T2 = 48224
PR_RSTD = 48736

GW1 = 33792
HT = 41984
GTR = 46166
TMP8 = 46176
GT2 = 46208
GT128 = 48384

MWB = (33792, 37888)
FSQ = 41984
FR_MEAN = 43008
FR_T1 = 43520
FR_T2 = 44032
FR_RSTD = 44544
OUT2 = (45056, 45568)
UF32 = (49152, 49664)

MW_B2 = BUFB + 5 * 2048
MW0 = dict(LW=(38912,), LT=(MW_B2, MW_B2 + 1024, MW_B2 + 2048),
           ACC=MW_B2 + 3072, ACCH=MW_B2 + 4096, ACCL=MW_B2 + 4608,
           w1w=1024, w2w=1024)
MW1 = dict(LW=(33792, 35840), LT=(37888, 39936, 41984, 44032),
           ACC=46080, ACCH=48128, ACCL=49152, w1w=2048, w2w=1024)
N_MW0 = DC * (F // 1024) + FC * (D // 1024)
N_MW1 = DC * (F // 2048) + FC * (D // 1024)
MW0_PP = (13, 13, 13, 13, 12, 0, 0, 0)
MW0_OFF = tuple(sum(MW0_PP[:p]) for p in range(NP))
XTP_ = (BUFB, BUFB + 4096)
XTH_ = (BUFB + 8192, BUFB + 10240)
XTL_ = (BUFB + 12288, BUFB + 14336)
UHI = BUFA
ULO = BUFA + 8192


class _MockInstr:
    def then_inc(self, h, k):
        return self


class _MockEng:
    def __getattr__(self, name):
        if name in ("wait_ge",):
            return lambda *a, **k: None
        return lambda *a, **k: _MockInstr()


_MOCK = _MockEng()


class Sem:
    def __init__(self, h):
        self.h = h
        self.n = 0
        self.e = 0


class Builder:
    def __init__(self, stage="full", npairs=None):
        self.stage = stage
        self.npairs = npairs
        nc = bass.Bass("TRN2", target_bir_lowering=False, debug=False)
        self.nc = nc
        self.T = {}
        self.mock = False

        self.d_xt = nc.dram_tensor("xt", [B, D, S], F32, kind="ExternalInput")
        self.d_wqkv = nc.dram_tensor("wqkv", [NP, 128, 3072], F32, kind="ExternalInput")
        self.d_pwt = nc.dram_tensor("pwt", [128, 8192], F32, kind="ExternalInput")
        self.d_gw1t = nc.dram_tensor("gw1t", [128, 8192], F32, kind="ExternalInput")
        self.d_misc = nc.dram_tensor("misc", [128, MISC_W], F32, kind="ExternalInput")
        self.d_w1t = nc.dram_tensor("w1t", [D, F], F32, kind="ExternalInput")
        self.d_tw1t = nc.dram_tensor("tw1t", [E, D, F], F32, kind="ExternalInput")
        self.d_w2t = nc.dram_tensor("w2t", [F, D], F32, kind="ExternalInput")
        self.d_tw2t = nc.dram_tensor("tw2t", [E, F, D], F32, kind="ExternalInput")
        self.d_out = nc.dram_tensor("outt", [B, D, S], F32, kind="ExternalOutput")
        mwkind = "ExternalOutput" if stage == "mw" else "Internal"
        BF16 = mybir.dt.bfloat16
        self.d_xtp = nc.dram_tensor("xtp_scr", [B, D, S], F32, kind=mwkind)
        self.d_mw1h = nc.dram_tensor("mw1h_scr", [B, D, F], BF16, kind=mwkind)
        self.d_mw1l = nc.dram_tensor("mw1l_scr", [B, D, F], BF16, kind=mwkind)
        self.d_mw2h = nc.dram_tensor("mw2h_scr", [B, DC, FC, 128, 128], BF16, kind=mwkind)
        self.d_mw2l = nc.dram_tensor("mw2l_scr", [B, DC, FC, 128, 128], BF16, kind=mwkind)
        if stage in ("attn", "ln1", "ln1x", "probe", "gate", "fulldbg"):
            self.d_dbg = nc.dram_tensor("dbg", [B, 128, 16384], F32, kind="ExternalOutput")

    def ar2(self, p0, np_, col, w):
        return bass.AP(self.AR, p0 * NCOLS + col, [[NCOLS, np_], [1, w]])

    def ar3(self, col, dims):
        return bass.AP(self.AR, col, [[NCOLS, 128]] + dims)

    def psa(self, p0, np_, col, w):
        return bass.AP(self.PSA, p0 * 2048 + col, [[2048, np_], [1, w]])

    def psb(self, p0, np_, col, w):
        return bass.AP(self.PSB, p0 * 2048 + col, [[2048, np_], [1, w]])

    def tag(self, key, sem, instr, k=1):
        if self.mock:
            sem.n += k
            assert key not in self.T, key
            self.T[key] = (sem, sem.n)
        else:
            sem.e += k
            assert self.T[key] == (sem, sem.e), (key, self.T[key], sem.e)
            instr.then_inc(sem.h, k)

    def tagn(self, sem, instr, k=1):
        if self.mock:
            sem.n += k
        else:
            sem.e += k
            instr.then_inc(sem.h, k)

    def alias(self, key2, sem):
        if self.mock:
            self.T[key2] = (sem, sem.n)

    def wt(self, eng, key):
        if self.mock:
            return
        sem, t = self.T[key]
        if id(sem) in self.dma_sems:
            sv = self.sealed.get(id(sem), 0)
            if sv >= t:
                t = sv
        eng.wait_ge(sem.h, t)

    def emit_block(self, sections):
        self.sealed = {id(sem): sem.n for sem in self.sems.values()}
        self.mock = True
        for _, fn in sections:
            fn(_MOCK)
        self.mock = False
        with self.nc.Block() as blk:
            for name, fn in sections:
                getattr(blk, name)(fn)

    def build(self):
        nc = self.nc
        import contextlib
        with contextlib.ExitStack() as ctx:
            AR = ctx.enter_context(nc.sbuf_tensor("AR", [128, NCOLS], F32))
            PSA = ctx.enter_context(nc.psum_tensor("PSA", [128, 2048], F32))
            PSB = ctx.enter_context(nc.psum_tensor("PSB", [128, 2048], F32))
            self.AR, self.PSA, self.PSB = AR, PSA, PSB
            names = ["SG", "SD", "MM", "AC", "DV", "MISC", "XT", "WBS", "PWT",
                     "GW1", "XTPL", "MW1L0", "MW1L1", "MW2L0", "MW2L1",
                     "LW0", "LW1", "LT0", "LT1", "LT2", "LT3", "LT4", "LT5"]
            sems = {n: Sem(ctx.enter_context(nc.semaphore(n))) for n in names}
            self.sems = sems
            self.SG, self.SD = sems["SG"], sems["SD"]
            self.MM, self.AC, self.DV = sems["MM"], sems["AC"], sems["DV"]
            self.S_MISC, self.S_XT, self.S_WB = sems["MISC"], sems["XT"], sems["WBS"]
            self.S_PWT, self.S_GW1, self.S_XTPL = sems["PWT"], sems["GW1"], sems["XTPL"]
            self.S_MW1L = (sems["MW1L0"], sems["MW1L1"])
            self.S_MW2L = (sems["MW2L0"], sems["MW2L1"])
            self.S_LW = (sems["LW0"], sems["LW1"])
            self.S_LT = tuple(sems["LT%d" % k] for k in range(6))
            self.dma_sems = {id(sems[n]) for n in names if n not in ("MM", "AC", "DV")}
            self.sealed = {}

            self.prologue()
            if self.stage == "ln1x":
                for p in range(NP):
                    self.a_pair(0, p)
                self.a_fin(0)
                for p in range(NP):
                    self.a_pair(1, p)
                self.a_fin(1)
                self.p_phase(1)
                self.dbg_dump(1, "ln1")
                self.epilogue()
                return nc
            if self.stage == "probe":
                for p in range(NP if self.npairs is None else self.npairs):
                    self.a_pair(0, p)
                self.probe_dump()
                self.epilogue()
                return nc
            for b in range(B):
                for p in range(NP):
                    self.a_pair(b, p)
                self.a_fin(b)
                if self.stage == "attn":
                    self.dbg_dump(b, "attn")
                    continue
                self.p_phase(b)
                if self.stage == "ln1":
                    self.dbg_dump(b, "ln1")
                    continue
                self.g_phase(b)
            if self.stage == "gate":
                self.gate_dump()
            if self.stage == "fulldbg":
                self.gate_dump(slot=1)
            if self.stage not in ("attn", "ln1", "gate"):
                if self.stage == "mw":
                    self.mw_standalone(0, N_MW0)
                    self.mw_standalone(N_MW0, N_MW0 + N_MW1)
                    self.mw_wait()
                    self.epilogue()
                    return nc
                self.mw_standalone(N_MW0, N_MW0 + N_MW1)
                prev = [None]
                for b in range(B):
                    for tt in range(TT):
                        self.f_block(b, tt, prev[0])
                        prev[0] = (b, tt)
                if self.stage == "fulldbg":
                    self.gate_dump()
            self.epilogue()
        return nc

    def prologue(self):
        def f_sync(eng):
            self.tag(("misc",), self.S_MISC, eng.dma_start(
                out=self.ar2(0, 128, 0, MISC_W),
                in_=bass.AP(self.d_misc, 0, [[MISC_W, 128], [1, MISC_W]])), 16)
            self.tag(("xt", 0), self.S_XT, eng.dma_start(
                out=self.ar3(BUFA, [[2048, DC], [1, 2048]]),
                in_=bass.AP(self.d_xt, 0, [[S, 128], [128 * S, DC], [1, S]])), 16)
            self.tag(("wb", 0, 0), self.S_WB, eng.dma_start(
                out=self.ar2(0, 128, WB, 3072),
                in_=bass.AP(self.d_wqkv, 0, [[3072, 128], [1, 3072]])), 16)

        self.emit_block([("sync", f_sync)])

    def a_pair(self, b, p):
        AR = self.AR
        inj = (b == 1 and self.stage in ("full", "fulldbg"))
        n_u, off_u = (MW0_PP[p], MW0_OFF[p]) if inj else (0, 0)

        def prev_it(h, it, back=1):
            g = h * 16 + it - back
            if g < 0:
                return None
            return (g // 16, g % 16)

        def f_sync(eng):
            if b > 0 and p == 0:
                self.tag(("wb", b, 0), self.S_WB, eng.dma_start(
                    out=self.ar2(0, 128, WB, 3072),
                    in_=bass.AP(self.d_wqkv, 0, [[3072, 128], [1, 3072]])), 16)
            if p + 1 < NP:
                self.wt(eng, ("vmm", b, p, 15))
                self.tag(("wb", b, p + 1), self.S_WB, eng.dma_start(
                    out=self.ar2(0, 128, WB, 3072),
                    in_=bass.AP(self.d_wqkv, (p + 1) * 128 * 3072,
                                [[3072, 128], [1, 3072]])), 16)

        def f_pe(eng):
            self.wt(eng, ("wb", b, p))
            self.wt(eng, ("misc",))
            self.wt(eng, ("xt", b))
            if ("exp", b, p - 1, 1, 15) in self.T:
                self.wt(eng, ("exp", b, p - 1, 1, 15))
                self.wt(eng, ("evB", b, p - 1, 1, 15))
                self.wt(eng, ("oev", b, p - 1, 1, 15))
            elif ("exp", b - 1, NP - 1, 1, 15) in self.T:
                self.wt(eng, ("exp", b - 1, NP - 1, 1, 15))
                self.wt(eng, ("evB", b - 1, NP - 1, 1, 15))
                self.wt(eng, ("oev", b - 1, NP - 1, 1, 15))

            tp = eng.transpose(self.psb(0, 1, 0, 128),
                               AR[0:128, C_BQKV + 3 * p + 2: C_BQKV + 3 * p + 3],
                               AR[0:128, C_ID: C_ID + 128])
            self.tag(("vbrow_mm", b, p), self.MM, tp)

            for role, colw, bank0 in (("q", 0, 0), ("k", 128, 1024)):
                for tt in range(TT):
                    if tt >= 2:
                        self.wt(eng, (role + "ev", b, p, tt - 2))
                    ps = self.psa(0, 128, bank0 + (tt % 2) * 512, 512)
                    for dc in range(DC):
                        ins = eng.matmul(
                            ps,
                            AR[0:128, WB + dc * 384 + colw: WB + dc * 384 + colw + 128],
                            AR[0:128, BUFA + dc * 2048 + tt * 512: BUFA + dc * 2048 + tt * 512 + 512],
                            start=(dc == 0), stop=(dc == DC - 1))
                    self.tag((role + "mm", b, p, tt), self.MM, ins)

            self.wt(eng, ("vbrow", b, p))
            for t2c in range(16):
                if t2c >= 4:
                    self.wt(eng, ("vev", b, p, t2c - 4))
                ps = self.psb(0, 128, (t2c % 4) * 512, 128)
                for dc in range(DC):
                    eng.matmul(
                        ps,
                        AR[0:128, BUFA + dc * 2048 + t2c * 128: BUFA + dc * 2048 + t2c * 128 + 128],
                        AR[0:128, WB + dc * 384 + 256: WB + dc * 384 + 384],
                        start=(dc == 0), stop=False)
                ins = eng.matmul(ps, AR[0:1, C_ONES: C_ONES + 128],
                                 AR[0:1, VBR: VBR + 128], start=False, stop=True)
                self.tag(("vmm", b, p, t2c), self.MM, ins)

            for h in range(2):
                base = h * 64
                for it in range(17):
                    k = it - 1
                    if it >= 1:
                        par = k % 2
                        self.wt(eng, ("norm", b, p, h, k))
                        if (h, k) == (0, 0):
                            self.wt(eng, ("vev", b, p, 15))
                        pv1b = prev_it(h, k, 1)
                        if pv1b is not None:
                            self.wt(eng, ("evB", b, p, pv1b[0], pv1b[1]))
                        for j in range(8):
                            ins = eng.matmul(
                                self.psb(0, 128, j * 128, 128),
                                AR[0:128, PT[par] + j * 128: PT[par] + j * 128 + 128],
                                AR[0:128, C_ID: C_ID + 128],
                                start=(j % 4 == 0), stop=(j % 4 == 3),
                                is_transpose=True)
                        self.tag(("tA", b, p, h, k), self.MM, ins)
                    if it < 16:
                        pv = prev_it(h, it)
                        if pv is None:
                            self.wt(eng, ("kev", b, p, TT - 1))
                            self.wt(eng, ("vev", b, p, 15))
                        else:
                            self.wt(eng, ("exp", b, p, pv[0], pv[1]))
                        for ns in range(4):
                            ins = eng.matmul(
                                self.psa(0, 128, ns * 512, 512),
                                AR[base: base + 64, QT + it * 128: QT + it * 128 + 128],
                                AR[base: base + 64, KT + ns * 512: KT + ns * 512 + 512],
                                start=True, stop=True)
                            self.tag(("smmp", b, p, h, it, ns), self.MM, ins)
                        self.alias(("smm", b, p, h, it), self.MM)
                    if it >= 1:
                        par = k % 2
                        self.wt(eng, ("evA", b, p, h, k))
                        for j in range(8, 16):
                            col = 1536 + (j - 8) * 128 if j < 12 else (j - 12) * 128
                            ins = eng.matmul(
                                self.psb(0, 128, col, 128),
                                AR[0:128, PT[par] + j * 128: PT[par] + j * 128 + 128],
                                AR[0:128, C_ID: C_ID + 128],
                                start=(j % 4 == 0), stop=(j % 4 == 3),
                                is_transpose=True)
                        self.tag(("tB", b, p, h, k), self.MM, ins)
                        pv1 = prev_it(h, k, 1)
                        if pv1 is not None:
                            self.wt(eng, ("oev", b, p, pv1[0], pv1[1]))
                        for t2c in range(8):
                            ins = eng.matmul(
                                bass.AP(self.PSB, 1024, [[2048, 64], [1, 128]]),
                                AR[0:128, VT + t2c * 128 + base: VT + t2c * 128 + base + 64],
                                AR[0:128, STRIP + t2c * 128: STRIP + t2c * 128 + 128],
                                start=(t2c == 0), stop=False)
                        self.wt(eng, ("evB", b, p, h, k))
                        for t2c in range(8, 16):
                            ins = eng.matmul(
                                bass.AP(self.PSB, 1024, [[2048, 64], [1, 128]]),
                                AR[0:128, VT + t2c * 128 + base: VT + t2c * 128 + base + 64],
                                AR[0:128, STRIPB + (t2c - 8) * 128: STRIPB + (t2c - 8) * 128 + 128],
                                start=False, stop=(t2c == 15))
                        self.tag(("pvB", b, p, h, k), self.MM, ins)

        def f_act(eng):
            for h in range(2):
                for it in range(17):
                    k = it - 1
                    if it >= 1:
                        self.wt(eng, ("tA", b, p, h, k))
                        ins = eng.activation(
                            out=AR[0:128, STRIP: STRIP + 1024],
                            in_=self.psb(0, 128, 0, 1024),
                            func=AF.Copy, bias=0.0, scale=1.0)
                        self.tag(("evA", b, p, h, k), self.AC, ins)
                        self.wt(eng, ("tB", b, p, h, k))
                        eng.activation(
                            out=AR[0:128, STRIPB: STRIPB + 512],
                            in_=self.psb(0, 128, 1536, 512),
                            func=AF.Copy, bias=0.0, scale=1.0)
                        ins = eng.activation(
                            out=AR[0:128, STRIPB + 512: STRIPB + 1024],
                            in_=self.psb(0, 128, 0, 512),
                            func=AF.Copy, bias=0.0, scale=1.0)
                        self.tag(("evB", b, p, h, k), self.AC, ins)
                    if it < 16:
                        self.wt(eng, ("smm", b, p, h, it))
                        self.wt(eng, ("bias", b, p, h, it))
                        ins = eng.activation(
                            out=AR[0:128, PT[it % 2]: PT[it % 2] + 2048],
                            in_=self.psa(0, 128, 0, 2048),
                            func=AF.Exp,
                            bias=AR[0:128, NMC + 1: NMC + 2],
                            scale=-1.0,
                            accum_out=AR[0:128, NMC + 2: NMC + 3])
                        self.tag(("exp", b, p, h, it), self.AC, ins)
                        self.tag(("sums", b, p, h, it), self.AC, eng.drain())

        def f_vec(eng):
            nstep = n_u * (E + 1)
            ninj = [0]

            def pace(itn):
                tgt = min(nstep, (itn * nstep + 33) // 34)
                while ninj[0] < tgt:
                    gi = off_u + ninj[0] // (E + 1)
                    e = ninj[0] % (E + 1)
                    if e < E:
                        self.mw_stt_step(eng, gi, e)
                    else:
                        self.mw_split(eng, gi)
                    ninj[0] += 1

            self.wt(eng, ("vbrow_mm", b, p))
            ins = eng.tensor_copy(AR[0:1, VBR: VBR + 128], self.psb(0, 1, 0, 128))
            self.tag(("vbrow", b, p), self.DV, ins)
            for role, bank0, bcol in (("q", 0, 0), ("k", 1024, 1)):
                dstc = QT if role == "q" else KT
                for tt in range(TT):
                    self.wt(eng, (role + "mm", b, p, tt))
                    if role == "q":
                        ins = eng.tensor_scalar(
                            out=AR[0:128, dstc + tt * 512: dstc + tt * 512 + 512],
                            in0=self.psa(0, 128, bank0 + (tt % 2) * 512, 512),
                            scalar1=AR[0:128, C_BQKV + 3 * p + bcol: C_BQKV + 3 * p + bcol + 1],
                            scalar2=-0.125, op0=ALU.add, op1=ALU.mult)
                    else:
                        ins = eng.tensor_scalar_add(
                            AR[0:128, dstc + tt * 512: dstc + tt * 512 + 512],
                            self.psa(0, 128, bank0 + (tt % 2) * 512, 512),
                            AR[0:128, C_BQKV + 3 * p + bcol: C_BQKV + 3 * p + bcol + 1])
                    self.tag((role + "ev", b, p, tt), self.DV, ins)
            for t2c in range(16):
                self.wt(eng, ("vmm", b, p, t2c))
                ins = eng.tensor_copy(
                    AR[0:128, VT + t2c * 128: VT + t2c * 128 + 128],
                    self.psb(0, 128, (t2c % 4) * 512, 128))
                self.tag(("vev", b, p, t2c), self.DV, ins)
            for h in range(2):
                for it in range(17):
                    if it < 16:
                        for ns in range(4):
                            self.wt(eng, ("smmp", b, p, h, it, ns))
                            eng.tensor_reduce(
                                AR[0:128, NMC + 4 + ns: NMC + 5 + ns],
                                self.psa(0, 128, ns * 512, 512),
                                axis=AX.X, op=ALU.min)
                        eng.drain()
                        eng.tensor_reduce(
                            AR[0:128, NMC + 1: NMC + 2],
                            AR[0:128, NMC + 4: NMC + 8],
                            axis=AX.X, op=ALU.min)
                        self.tag(("bias", b, p, h, it), self.DV, eng.drain())
                        self.wt(eng, ("sums", b, p, h, it))
                        eng.reciprocal(AR[0:128, NMC + 3: NMC + 4],
                                       AR[0:128, NMC + 2: NMC + 3])
                        eng.drain()
                        ins = eng.tensor_scalar_mul(
                            AR[0:128, PT[it % 2]: PT[it % 2] + 2048],
                            AR[0:128, PT[it % 2]: PT[it % 2] + 2048],
                            AR[0:128, NMC + 3: NMC + 4])
                        self.tag(("norm", b, p, h, it), self.DV, ins)
                    if it >= 1:
                        k = it - 1
                        self.wt(eng, ("pvB", b, p, h, k))
                        if h == 1 and k == 0:
                            if ("oshift", b, p - 1) in self.T:
                                self.wt(eng, ("oshift", b, p - 1))
                            elif ("oshift", b - 1, NP - 1) in self.T:
                                self.wt(eng, ("oshift", b - 1, NP - 1))
                        if h == 0:
                            ins = eng.tensor_copy(
                                AR[0:64, BUFB + p * 2048 + k * 128: BUFB + p * 2048 + k * 128 + 128],
                                bass.AP(self.PSB, 1024, [[2048, 64], [1, 128]]))
                        else:
                            ins = eng.tensor_copy(
                                AR[0:64, STAGE + k * 128: STAGE + k * 128 + 128],
                                bass.AP(self.PSB, 1024, [[2048, 64], [1, 128]]))
                        self.tag(("oev", b, p, h, k), self.DV, ins)
                    pace(h * 17 + it + 1)

        def f_gps(eng):
            if n_u:
                self.mw_load(eng, off_u)
                for k in range(1, n_u):
                    self.mw_load(eng, off_u + k)
                    self.mw_write(eng, off_u + k - 1)
                self.mw_write(eng, off_u + n_u - 1)
            self.wt(eng, ("oev", b, p, 1, 15))
            ins = eng.dma_start(
                out=AR[64:128, BUFB + p * 2048: BUFB + p * 2048 + 2048],
                in_=AR[0:64, STAGE: STAGE + 2048])
            self.tag(("oshift", b, p), self.SG, ins, 16)

        self.emit_block([("sync", f_sync), ("tensor", f_pe), ("scalar", f_act),
                         ("vector", f_vec), ("gpsimd", f_gps)])

    def a_fin(self, b):
        AR = self.AR

        def f_sync(eng):
            if self.stage != "attn":
                self.tag(("pwt", b), self.S_PWT, eng.dma_start(
                    out=self.ar2(0, 128, PWT, 8192),
                    in_=bass.AP(self.d_pwt, 0, [[8192, 128], [1, 8192]])), 16)
                if self.stage == "ln1x" and b == 0:
                    self.tag(("xt", 1), self.S_XT, eng.dma_start(
                        out=self.ar3(BUFA, [[2048, DC], [1, 2048]]),
                        in_=bass.AP(self.d_xt, D * S, [[S, 128], [128 * S, DC], [1, S]])), 16)
            elif b == 0:
                self.tag(("xt", 1), self.S_XT, eng.dma_start(
                    out=self.ar3(BUFA, [[2048, DC], [1, 2048]]),
                    in_=bass.AP(self.d_xt, D * S, [[S, 128], [128 * S, DC], [1, S]])), 16)

        self.emit_block([("sync", f_sync)])

    def p_phase(self, b):
        AR = self.AR

        def f_pe(eng):
            self.wt(eng, ("pwt", b))
            for pp in range(NP):
                self.wt(eng, ("oev", b, pp, 0, 15))
                self.wt(eng, ("oshift", b, pp))
            for tt in range(TT):
                for dc in range(DC):
                    k = tt * DC + dc
                    if k >= 2:
                        self.wt(eng, ("zev", b, k - 2))
                    ps = self.psa(0, 128, (dc % 2) * 512, 512)
                    for pc in range(NP):
                        ins = eng.matmul(
                            ps,
                            AR[0:128, PWT + pc * 1024 + dc * 128: PWT + pc * 1024 + dc * 128 + 128],
                            AR[0:128, BUFB + pc * 2048 + tt * 512: BUFB + pc * 2048 + tt * 512 + 512],
                            start=(pc == 0), stop=(pc == NP - 1))
                    self.tag(("ymm", b, k), self.MM, ins)
                for dc in range(DC):
                    self.wt(eng, ("zev", b, tt * DC + dc))
                    eng.matmul(
                        self.psb(0, 1, 0, 512),
                        AR[0:128, C_ONES: C_ONES + 1],
                        AR[0:128, PZT + dc * 512: PZT + dc * 512 + 512],
                        start=(dc == 0), stop=(dc == DC - 1))
                    self.wt(eng, ("sq", b, tt, dc))
                    ins2 = eng.matmul(
                        self.psb(0, 1, 512, 512),
                        AR[0:128, C_ONES: C_ONES + 1],
                        AR[0:128, PSQ + (dc % 2) * 512: PSQ + (dc % 2) * 512 + 512],
                        start=(dc == 0), stop=(dc == DC - 1))
                    self.tag(("ssq", b, tt, dc), self.MM, ins2)
                self.wt(eng, ("rows", b, tt))
                eng.matmul(self.psb(0, 128, 1024, 512),
                           AR[0:1, C_ONES: C_ONES + 128],
                           AR[0:1, PR_MEAN: PR_MEAN + 512], start=True, stop=True)
                i2 = eng.matmul(self.psb(0, 128, 1536, 512),
                                AR[0:1, C_ONES: C_ONES + 128],
                                AR[0:1, PR_RSTD: PR_RSTD + 512], start=True, stop=True)
                self.tag(("bcln", b, tt), self.MM, i2)

        def f_vec(eng):
            for tt in range(TT):
                for dc in range(DC):
                    k = tt * DC + dc
                    self.wt(eng, ("ymm", b, k))
                    if tt > 0:
                        self.wt(eng, ("sq", b, tt - 1, dc))
                    ins = eng.scalar_tensor_tensor(
                        out=AR[0:128, PZT + dc * 512: PZT + dc * 512 + 512],
                        in0=self.psa(0, 128, (dc % 2) * 512, 512),
                        scalar=AR[0:128, C_PJB + dc: C_PJB + dc + 1],
                        in1=AR[0:128, BUFA + dc * 2048 + tt * 512: BUFA + dc * 2048 + tt * 512 + 512],
                        op0=ALU.add, op1=ALU.add)
                    self.tag(("zev", b, k), self.DV, ins)
                self.wt(eng, ("ssq", b, tt, DC - 1))
                eng.tensor_scalar_mul(AR[0:1, PR_MEAN: PR_MEAN + 512],
                                      self.psb(0, 1, 0, 512), 1.0 / 1024.0)
                eng.tensor_scalar_mul(AR[0:1, PR_T1: PR_T1 + 512],
                                      self.psb(0, 1, 512, 512), 1.0 / 1024.0)
                eng.tensor_tensor(out=AR[0:1, PR_T2: PR_T2 + 512],
                                  in0=AR[0:1, PR_MEAN: PR_MEAN + 512],
                                  in1=AR[0:1, PR_MEAN: PR_MEAN + 512], op=ALU.mult)
                ins = eng.tensor_tensor(out=AR[0:1, PR_T1: PR_T1 + 512],
                                        in0=AR[0:1, PR_T1: PR_T1 + 512],
                                        in1=AR[0:1, PR_T2: PR_T2 + 512], op=ALU.subtract)
                self.tag(("var", b, tt), self.DV, ins)
                self.wt(eng, ("sdrow", b, tt))
                eng.reciprocal(AR[0:1, PR_RSTD: PR_RSTD + 512],
                               AR[0:1, PR_T2: PR_T2 + 512])
                self.tag(("rows", b, tt), self.DV, eng.drain())
                self.wt(eng, ("bcln", b, tt))
                for dc in range(DC):
                    eng.tensor_tensor(
                        out=AR[0:128, BUFB + dc * 2048 + tt * 512: BUFB + dc * 2048 + tt * 512 + 512],
                        in0=AR[0:128, PZT + dc * 512: PZT + dc * 512 + 512],
                        in1=self.psb(0, 128, 1024, 512), op=ALU.subtract)
                    ins = eng.tensor_tensor(
                        out=AR[0:128, BUFB + dc * 2048 + tt * 512: BUFB + dc * 2048 + tt * 512 + 512],
                        in0=AR[0:128, BUFB + dc * 2048 + tt * 512: BUFB + dc * 2048 + tt * 512 + 512],
                        in1=self.psb(0, 128, 1536, 512), op=ALU.mult)
                self.tag(("xtp", b, tt), self.DV, ins)

        def f_act(eng):
            for tt in range(TT):
                for dc in range(DC):
                    self.wt(eng, ("zev", b, tt * DC + dc))
                    if tt * DC + dc >= 2:
                        self.wt(eng, ("ssq", b, tt, dc - 2) if dc >= 2
                                else ("ssq", b, tt - 1, DC + dc - 2))
                    ins = eng.activation(
                        out=AR[0:128, PSQ + (dc % 2) * 512: PSQ + (dc % 2) * 512 + 512],
                        in_=AR[0:128, PZT + dc * 512: PZT + dc * 512 + 512],
                        func=AF.Square, bias=0.0, scale=1.0)
                    self.tag(("sq", b, tt, dc), self.AC, ins)
                self.wt(eng, ("var", b, tt))
                ins = eng.activation(out=AR[0:1, PR_T2: PR_T2 + 512],
                                     in_=AR[0:1, PR_T1: PR_T1 + 512],
                                     func=AF.Sqrt,
                                     bias=AR[0:1, C_EPS: C_EPS + 1], scale=1.0)
                self.tag(("sdrow", b, tt), self.AC, ins)

        def f_gps(eng):
            for tt in range(TT):
                self.wt(eng, ("xtp", b, tt))
                ins = eng.dma_start(
                    out=bass.AP(self.d_xtp, b * D * S + tt * 512,
                                [[S, 128], [128 * S, DC], [1, 512]]),
                    in_=bass.AP(AR, BUFB + tt * 512, [[NCOLS, 128], [2048, DC], [1, 512]]))
                self.tag(("xtpw", b, tt), self.SG, ins, 16)

        self.emit_block([("tensor", f_pe), ("vector", f_vec), ("scalar", f_act),
                         ("gpsimd", f_gps)])

    def g_phase(self, b):
        AR = self.AR

        def f_sync(eng):
            self.tag(("gw1", b), self.S_GW1, eng.dma_start(
                out=self.ar2(0, 128, GW1, 8192),
                in_=bass.AP(self.d_gw1t, 0, [[8192, 128], [1, 8192]])), 16)
            if b == 0:
                self.tag(("xt", 1), self.S_XT, eng.dma_start(
                    out=self.ar3(BUFA, [[2048, DC], [1, 2048]]),
                    in_=bass.AP(self.d_xt, D * S, [[S, 128], [128 * S, DC], [1, S]])), 16)

        def f_pe(eng):
            self.wt(eng, ("gw1", b))
            for tt in range(TT):
                for oc in range(DC):
                    k = tt * DC + oc
                    if k >= 2:
                        self.wt(eng, ("hev", b, k - 2))
                    ps = self.psa(0, 128, (oc % 2) * 512, 512)
                    for dc in range(DC):
                        ins = eng.matmul(
                            ps,
                            AR[0:128, GW1 + dc * 1024 + oc * 128: GW1 + dc * 1024 + oc * 128 + 128],
                            AR[0:128, BUFB + dc * 2048 + tt * 512: BUFB + dc * 2048 + tt * 512 + 512],
                            start=(dc == 0), stop=(dc == DC - 1))
                    self.tag(("hmm", b, k), self.MM, ins)
                for oc in range(DC):
                    self.wt(eng, ("hev", b, tt * DC + oc))
                    ins = eng.matmul(
                        self.psb(0, 8, tt * 512, 512),
                        AR[0:128, C_GW2 + oc * 8: C_GW2 + oc * 8 + 8],
                        AR[0:128, HT + oc * 512: HT + oc * 512 + 512],
                        start=(oc == 0), stop=(oc == DC - 1))
                self.tag(("g2", b, tt), self.MM, ins)
            self.wt(eng, ("gevac", b))
            for c in range(16):
                ins = eng.matmul(self.psa(0, 128, 1024, 8),
                                 AR[0:8, GT2 + c * 128: GT2 + c * 128 + 128],
                                 AR[0:8, C_ID: C_ID + 8],
                                 start=(c == 0), stop=(c == 15), is_transpose=True)
            self.tag(("gtacc", b), self.MM, ins)
            self.wt(eng, ("g128", b))
            eng.matmul(self.psa(0, 1, 1536, 8),
                       AR[0:128, C_INV: C_INV + 1],
                       AR[0:128, GT128: GT128 + 8], start=True, stop=False)
            ins = eng.matmul(self.psa(0, 1, 1536, 8),
                             AR[0:1, C_ONES: C_ONES + 1],
                             AR[0:1, C_GB2R: C_GB2R + 8], start=False, stop=True)
            self.tag(("growmm", b), self.MM, ins)
            self.wt(eng, ("gtrow", b))
            ins = eng.matmul(self.psb(0, 128, 0, 8),
                             AR[0:1, C_ONES: C_ONES + 128],
                             AR[0:1, GTR: GTR + 8], start=True, stop=True)
            self.tag(("gbcmm", b), self.MM, ins)

        def f_act(eng):
            for tt in range(TT):
                for oc in range(DC):
                    k = tt * DC + oc
                    self.wt(eng, ("hmm", b, k))
                    ins = eng.activation(
                        out=AR[0:128, HT + oc * 512: HT + oc * 512 + 512],
                        in_=self.psa(0, 128, (oc % 2) * 512, 512),
                        func=AF.Relu, bias=AR[0:128, C_GB1 + oc: C_GB1 + oc + 1],
                        scale=1.0)
                    self.tag(("hev", b, k), self.AC, ins)
            for tt in range(TT):
                self.wt(eng, ("g2", b, tt))
            ins = eng.activation(out=AR[0:8, GT2: GT2 + 2048],
                                 in_=self.psb(0, 8, 0, 2048),
                                 func=AF.Copy, bias=0.0, scale=1.0)
            self.tag(("gevac", b), self.AC, ins)

        def f_vec(eng):
            self.wt(eng, ("gtacc", b))
            eng.tensor_copy(AR[0:128, GT128: GT128 + 8],
                            self.psa(0, 128, 1024, 8))
            self.tag(("g128", b), self.DV, eng.drain())
            self.wt(eng, ("growmm", b))
            eng.tensor_copy(AR[0:1, GTR: GTR + 8], self.psa(0, 1, 1536, 8))
            self.tag(("gtrow", b), self.DV, eng.drain())
            self.wt(eng, ("gbcmm", b))
            eng.tensor_copy(AR[0:128, GBC + 8 * b: GBC + 8 * b + 8],
                            self.psb(0, 128, 0, 8))
            self.tag(("gbc", b), self.DV, eng.drain())
            for nchunk, ctb, cbt, dst in ((32, C_TB1, C_B1T, MB1 + 32 * b),
                                          (8, C_TB2, C_B2T, MB2 + 8 * b)):
                for c in range(nchunk):
                    for e in range(E):
                        ins = eng.scalar_tensor_tensor(
                            out=AR[0:128, dst + c: dst + c + 1],
                            in0=AR[0:128, ctb + c * 8 + e: ctb + c * 8 + e + 1],
                            scalar=bass.AP(self.PSB, e, [[2048, 128], [1, 1]]),
                            in1=(AR[0:128, cbt + c: cbt + c + 1] if e == 0
                                 else AR[0:128, dst + c: dst + c + 1]),
                            op0=ALU.mult, op1=ALU.add)
            self.tag(("mbdone", b), self.DV, ins)

        self.emit_block([("sync", f_sync) if b == 0 else ("sync", f_sync),
                         ("tensor", f_pe), ("scalar", f_act), ("vector", f_vec)])

    def mw_cfg(self, gi):
        if gi < N_MW0:
            bb, L, ui = 0, MW0, gi
        else:
            bb, L, ui = 1, MW1, gi - N_MW0
        c1 = F // L["w1w"]
        if ui < DC * c1:
            kind, i, ch, w = "w1", ui // c1, ui % c1, L["w1w"]
        else:
            u2 = ui - DC * c1
            c2 = D // L["w2w"]
            kind, i, ch, w = "w2", u2 // c2, u2 % c2, L["w2w"]
        return bb, L, kind, i, ch, w

    def mw_load(self, eng, gi):
        bb, L, kind, i, ch, w = self.mw_cfg(gi)
        if kind == "w1":
            off, pitch = i * 128 * F + ch * w, F
            dw, dtw, tstep = self.d_w1t, self.d_tw1t, D * F
        else:
            off, pitch = i * 128 * D + ch * w, D
            dw, dtw, tstep = self.d_w2t, self.d_tw2t, F * D
        nlw = len(L["LW"])
        if gi >= nlw:
            self.wt(eng, ("mst", gi - nlw, 0))
        self.tag(("mlw", gi), self.S_LW[gi % nlw], eng.dma_start(
            out=self.ar2(0, 128, L["LW"][gi % nlw], w),
            in_=bass.AP(dw, off, [[pitch, 128], [1, w]])), 16)
        nlt = len(L["LT"])
        for e in range(E):
            if e >= nlt:
                self.wt(eng, ("mst", gi, e - nlt))
            elif gi >= 1:
                self.wt(eng, ("mst", gi - 1,
                              ((E - 1 - e % nlt) // nlt) * nlt + e % nlt))
            self.tag(("mlt", gi, e), self.S_LT[e % nlt], eng.dma_start(
                out=self.ar2(0, 128, L["LT"][e % nlt], w),
                in_=bass.AP(dtw, e * tstep + off, [[pitch, 128], [1, w]])), 16)

    def mw_stt_step(self, eng, gi, e):
        AR = self.AR
        bb, L, kind, i, ch, w = self.mw_cfg(gi)
        if e == 0:
            if gi in (0, N_MW0):
                self.wt(eng, ("gbc", bb))
            self.wt(eng, ("mlw", gi))
        nlt = len(L["LT"])
        self.wt(eng, ("mlt", gi, e))
        nlw = len(L["LW"])
        ins = eng.scalar_tensor_tensor(
            out=AR[0:128, L["ACC"]: L["ACC"] + w],
            in0=AR[0:128, L["LT"][e % nlt]: L["LT"][e % nlt] + w],
            scalar=AR[0:128, GBC + 8 * bb + e: GBC + 8 * bb + e + 1],
            in1=(AR[0:128, L["LW"][gi % nlw]: L["LW"][gi % nlw] + w] if e == 0
                 else AR[0:128, L["ACC"]: L["ACC"] + w]),
            op0=ALU.mult, op1=ALU.add)
        self.tag(("mst", gi, e), self.DV, ins)

    def mw_split(self, eng, gi):
        AR = self.AR
        bb, L, kind, i, ch, w = self.mw_cfg(gi)
        if gi >= 1:
            self.wt(eng, ("mwr", gi - 1))
        BF16 = mybir.dt.bfloat16
        hiv = AR[0:128, L["ACCH"]: L["ACCH"] + w // 2].bitcast(BF16)
        lov = AR[0:128, L["ACCL"]: L["ACCL"] + w // 2].bitcast(BF16)
        eng.tensor_copy(hiv, AR[0:128, L["ACC"]: L["ACC"] + w])
        eng.tensor_tensor(out=lov, in0=AR[0:128, L["ACC"]: L["ACC"] + w],
                          in1=hiv, op=ALU.subtract)
        self.tagn(self.DV, eng.drain())
        self.alias(("macc", gi), self.DV)

    def mw_stt(self, eng, gi):
        for e in range(E):
            self.mw_stt_step(eng, gi, e)
        self.mw_split(eng, gi)

    def mw_write(self, eng, gi):
        BF16 = mybir.dt.bfloat16
        bb, L, kind, i, ch, w = self.mw_cfg(gi)
        self.wt(eng, ("macc", gi))
        for dmat1, dmat2, reg in ((self.d_mw1h, self.d_mw2h, L["ACCH"]),
                                  (self.d_mw1l, self.d_mw2l, L["ACCL"])):
            if kind == "w1":
                ins = eng.dma_start(
                    out=bass.AP(dmat1, bb * D * F + i * 128 * F + ch * w,
                                [[F, 128], [1, w]]),
                    in_=bass.AP(self.AR, reg,
                                [[NCOLS, 128], [1, w // 2]]).bitcast(BF16))
            else:
                qn = w // 128
                ins = eng.dma_start(
                    out=bass.AP(dmat2, ((bb * DC + ch * qn) * FC + i) * 16384,
                                [[128, 128], [FC * 16384, qn], [1, 128]]),
                    in_=bass.AP(self.AR, reg,
                                [[NCOLS, 128], [64, qn], [1, 64]]).bitcast(BF16))
            self.tagn(self.SD, ins, 16)
        self.alias(("mwr", gi), self.SD)
        if gi == N_MW0 - 1:
            self.alias(("mwall", 0), self.SD)
        if gi == N_MW0 + N_MW1 - 1:
            self.alias(("mwall", 1), self.SD)

    def mw_standalone(self, lo, hi):
        def f_sync(eng):
            for gi in range(lo, hi):
                self.mw_load(eng, gi)

        def f_vec(eng):
            for gi in range(lo, hi):
                self.mw_stt(eng, gi)

        def f_act(eng):
            for gi in range(lo, hi):
                self.mw_write(eng, gi)

        self.emit_block([("sync", f_sync), ("vector", f_vec), ("scalar", f_act)])

    def f_block(self, b, tt, prev):
        AR = self.AR
        BF16 = mybir.dt.bfloat16
        xtp = XTP_[tt % 2]
        xth = XTH_[tt % 2]
        xtl = XTL_[tt % 2]

        def bfv(col, w32, dims=None):
            if dims is None:
                return bass.AP(AR, col, [[NCOLS, 128], [1, w32]]).bitcast(BF16)
            return bass.AP(AR, col, [[NCOLS, 128]] + dims).bitcast(BF16)

        def f_sync(eng):
            if tt == 0:
                self.wt(eng, ("mwall", b))
            self.wt(eng, ("xtpw", b, tt))
            pk = None
            if tt >= 2:
                pk = (b, tt - 2)
            elif b == 1:
                pk = (0, tt + 2)
            if pk is not None:
                self.wt(eng, ("outr", pk[0], pk[1]))
                self.wt(eng, ("fsd", pk[0], pk[1]))
                self.wt(eng, ("f1", pk[0], pk[1], 7))
            self.tag(("xtpl", b, tt), self.S_XTPL, eng.dma_start(
                out=bass.AP(AR, xtp, [[NCOLS, 128], [512, DC], [1, 512]]),
                in_=bass.AP(self.d_xtp, b * D * S + tt * 512,
                            [[S, 128], [128 * S, DC], [1, 512]])), 16)
            for fg in range(8):
                if fg >= 2:
                    self.wt(eng, ("f1", b, tt, fg - 2))
                elif prev is not None:
                    self.wt(eng, ("zmm", prev[0], prev[1], 6 + fg))
                for dmat, coff in ((self.d_mw1h, 0), (self.d_mw1l, 2048)):
                    for dc in range(DC):
                        ins = eng.dma_start(
                            out=bfv(MWB[fg % 2] + coff + dc * 256, 256),
                            in_=bass.AP(dmat, b * D * F + dc * 128 * F + fg * 512,
                                        [[F, 128], [1, 512]]))
                        self.tagn(self.S_MW1L[fg % 2], ins, 16)
                self.alias(("mw1l", b, tt, fg), self.S_MW1L[fg % 2])
            for hc in range(DC):
                if hc >= 2:
                    self.wt(eng, ("zmm", b, tt, hc - 2))
                else:
                    self.wt(eng, ("f1", b, tt, 6 + hc))
                for dmat, coff in ((self.d_mw2h, 0), (self.d_mw2l, 2048)):
                    for q in range(4):
                        ins = eng.dma_start(
                            out=bfv(MWB[hc % 2] + coff + q * 512, None,
                                    [[64, 8], [1, 64]]),
                            in_=bass.AP(dmat, ((b * DC + hc) * FC + q * 8) * 16384,
                                        [[128, 128], [16384, 8], [1, 128]]))
                        self.tagn(self.S_MW2L[hc % 2], ins, 16)
                self.alias(("mw2l", b, tt, hc), self.S_MW2L[hc % 2])

        def f_pe(eng):
            self.wt(eng, ("xsplit", b, tt))
            self.wt(eng, ("mbdone", b))
            for fg in range(8):
                self.wt(eng, ("mw1l", b, tt, fg))
                if fg >= 2:
                    for i in range(4):
                        self.wt(eng, ("gel", b, tt, (fg - 2) * 4 + i))
                if fg == 1 and prev is not None:
                    self.wt(eng, ("outr", prev[0], prev[1]))
                psd = self.PSA if fg % 2 == 0 else self.PSB
                mh = MWB[fg % 2]
                ml = MWB[fg % 2] + 2048
                _P3 = int(os.environ.get("P3", "3"))
                for dc in range(DC):
                    for i in range(4):
                        for pi, (mc, xc) in enumerate((((mh, xth), (mh, xtl), (ml, xth)))[:_P3]):
                            ins = eng.matmul(
                                bass.AP(psd, i * 512, [[2048, 128], [1, 512]]),
                                bfv(mc + dc * 256 + i * 64, 64),
                                bfv(xc + dc * 256, 256),
                                start=(dc == 0 and pi == 0),
                                stop=(dc == DC - 1 and pi == _P3 - 1))
                self.tag(("f1", b, tt, fg), self.MM, ins)
            for hc in range(DC):
                self.wt(eng, ("mw2l", b, tt, hc))
                if hc < 2:
                    self.wt(eng, ("gel", b, tt, 27))
                else:
                    self.wt(eng, ("z2", b, tt, hc - 2))
                mh = MWB[hc % 2]
                ml = MWB[hc % 2] + 2048
                _P3 = int(os.environ.get("P3", "3"))
                for fc in range(FC):
                    if fc % 4 == 0:
                        self.wt(eng, ("gel", b, tt, min(fc + 3, 31)))
                    for pi, (mc, uc) in enumerate((((mh, UHI), (mh, ULO), (ml, UHI)))[:_P3]):
                        ins = eng.matmul(
                            bass.AP(self.PSA, (hc % 2) * 512, [[2048, 128], [1, 512]]),
                            bfv(mc + fc * 64, 64),
                            bfv(uc + fc * 256, 256),
                            start=(fc == 0 and pi == 0),
                            stop=(fc == FC - 1 and pi == _P3 - 1))
                self.tag(("zmm", b, tt, hc), self.MM, ins)
            for hc in range(DC):
                self.wt(eng, ("z2", b, tt, hc))
                eng.matmul(self.psb(0, 1, 0, 512),
                           AR[0:128, C_ONES: C_ONES + 1],
                           AR[0:128, xtp + hc * 512: xtp + hc * 512 + 512],
                           start=(hc == 0), stop=(hc == DC - 1))
                self.wt(eng, ("fsq", b, tt, hc))
                ins2 = eng.matmul(self.psb(0, 1, 512, 512),
                                  AR[0:128, C_ONES: C_ONES + 1],
                                  AR[0:128, FSQ + (hc % 2) * 512: FSQ + (hc % 2) * 512 + 512],
                                  start=(hc == 0), stop=(hc == DC - 1))
                self.tag(("fssq", b, tt, hc), self.MM, ins2)
            self.wt(eng, ("frows", b, tt))
            eng.matmul(self.psb(0, 128, 1024, 512),
                       AR[0:1, C_ONES: C_ONES + 128],
                       AR[0:1, FR_MEAN: FR_MEAN + 512], start=True, stop=True)
            i2 = eng.matmul(self.psb(0, 128, 1536, 512),
                            AR[0:1, C_ONES: C_ONES + 128],
                            AR[0:1, FR_RSTD: FR_RSTD + 512], start=True, stop=True)
            self.tag(("fbc", b, tt), self.MM, i2)

        def f_act(eng):
            for fg in range(8):
                psd = self.PSA if fg % 2 == 0 else self.PSB
                self.wt(eng, ("f1", b, tt, fg))
                for i in range(4):
                    fc = fg * 4 + i
                    if fc >= 2:
                        self.wt(eng, ("gel", b, tt, fc - 2))
                    ins = eng.activation(
                        out=AR[0:128, UF32[fc % 2]: UF32[fc % 2] + 512],
                        in_=bass.AP(psd, i * 512, [[2048, 128], [1, 512]]),
                        func=AF.Gelu_apprx_tanh,
                        bias=AR[0:128, MB1 + 32 * b + fc: MB1 + 32 * b + fc + 1],
                        scale=1.0)
                    self.tag(("gelA", b, tt, fc), self.AC, ins)
            for hc in range(DC):
                self.wt(eng, ("z2", b, tt, hc))
                if hc >= 2:
                    self.wt(eng, ("fssq", b, tt, hc - 2))
                ins = eng.activation(
                    out=AR[0:128, FSQ + (hc % 2) * 512: FSQ + (hc % 2) * 512 + 512],
                    in_=AR[0:128, xtp + hc * 512: xtp + hc * 512 + 512],
                    func=AF.Square, bias=0.0, scale=1.0)
                self.tag(("fsq", b, tt, hc), self.AC, ins)
            self.wt(eng, ("fvar", b, tt))
            ins = eng.activation(out=AR[0:1, FR_T2: FR_T2 + 512],
                                 in_=AR[0:1, FR_T1: FR_T1 + 512],
                                 func=AF.Sqrt,
                                 bias=AR[0:1, C_EPS: C_EPS + 1], scale=1.0)
            self.tag(("fsd", b, tt), self.AC, ins)

        def f_vec(eng):
            self.wt(eng, ("xtpl", b, tt))
            if prev is not None:
                ppk = (b, tt - 2) if tt >= 2 else ((0, tt + 2) if b == 1 else None)
                if ppk is not None:
                    self.wt(eng, ("f1", ppk[0], ppk[1], 7))
            eng.tensor_copy(bfv(xth, 2048), bass.AP(AR, xtp, [[NCOLS, 128], [1, 4096]]))
            eng.tensor_tensor(out=bfv(xtl, 2048),
                              in0=bass.AP(AR, xtp, [[NCOLS, 128], [1, 4096]]),
                              in1=bfv(xth, 2048), op=ALU.subtract)
            self.tag(("xsplit", b, tt), self.DV, eng.drain())
            for fc in range(FC):
                self.wt(eng, ("gelA", b, tt, fc))
                eng.tensor_copy(bfv(UHI + fc * 256, 256),
                                AR[0:128, UF32[fc % 2]: UF32[fc % 2] + 512])
                eng.tensor_tensor(out=bfv(ULO + fc * 256, 256),
                                  in0=AR[0:128, UF32[fc % 2]: UF32[fc % 2] + 512],
                                  in1=bfv(UHI + fc * 256, 256), op=ALU.subtract)
                self.tag(("gel", b, tt, fc), self.DV, eng.drain())
            for hc in range(DC):
                self.wt(eng, ("zmm", b, tt, hc))
                if prev is not None and hc == 0:
                    self.wt(eng, ("fsd", prev[0], prev[1]))
                ins = eng.scalar_tensor_tensor(
                    out=AR[0:128, xtp + hc * 512: xtp + hc * 512 + 512],
                    in0=bass.AP(self.PSA, (hc % 2) * 512, [[2048, 128], [1, 512]]),
                    scalar=AR[0:128, MB2 + 8 * b + hc: MB2 + 8 * b + hc + 1],
                    in1=AR[0:128, xtp + hc * 512: xtp + hc * 512 + 512],
                    op0=ALU.add, op1=ALU.add)
                self.tag(("z2", b, tt, hc), self.DV, ins)
            self.wt(eng, ("fssq", b, tt, DC - 1))
            eng.tensor_scalar_mul(AR[0:1, FR_MEAN: FR_MEAN + 512],
                                  self.psb(0, 1, 0, 512), 1.0 / 1024.0)
            eng.tensor_scalar_mul(AR[0:1, FR_T1: FR_T1 + 512],
                                  self.psb(0, 1, 512, 512), 1.0 / 1024.0)
            eng.tensor_tensor(out=AR[0:1, FR_T2: FR_T2 + 512],
                              in0=AR[0:1, FR_MEAN: FR_MEAN + 512],
                              in1=AR[0:1, FR_MEAN: FR_MEAN + 512], op=ALU.mult)
            ins = eng.tensor_tensor(out=AR[0:1, FR_T1: FR_T1 + 512],
                                    in0=AR[0:1, FR_T1: FR_T1 + 512],
                                    in1=AR[0:1, FR_T2: FR_T2 + 512], op=ALU.subtract)
            self.tag(("fvar", b, tt), self.DV, ins)
            self.wt(eng, ("fsd", b, tt))
            eng.reciprocal(AR[0:1, FR_RSTD: FR_RSTD + 512],
                           AR[0:1, FR_T2: FR_T2 + 512])
            self.tag(("frows", b, tt), self.DV, eng.drain())
            self.wt(eng, ("fbc", b, tt))
            for hc in range(DC):
                if hc >= 2:
                    self.wt(eng, ("outw", b, tt, hc - 2))
                elif prev is not None:
                    self.wt(eng, ("outw", prev[0], prev[1], 6 + hc))
                o2 = OUT2[hc % 2]
                eng.tensor_tensor(
                    out=AR[0:128, o2: o2 + 512],
                    in0=AR[0:128, xtp + hc * 512: xtp + hc * 512 + 512],
                    in1=self.psb(0, 128, 1024, 512), op=ALU.subtract)
                ins = eng.tensor_tensor(
                    out=AR[0:128, o2: o2 + 512],
                    in0=AR[0:128, o2: o2 + 512],
                    in1=self.psb(0, 128, 1536, 512), op=ALU.mult)
                self.tag(("outr", b, tt, hc), self.DV, ins)
            self.alias(("outr", b, tt), self.DV)

        def f_gps(eng):
            for hc in range(DC):
                self.wt(eng, ("outr", b, tt, hc))
                ins = eng.dma_start(
                    out=bass.AP(self.d_out, b * D * S + hc * 128 * S + tt * 512,
                                [[S, 128], [1, 512]]),
                    in_=bass.AP(AR, OUT2[hc % 2], [[NCOLS, 128], [1, 512]]))
                self.tag(("outw", b, tt, hc), self.SG, ins, 16)
            self.alias(("outw", b, tt), self.SG)

        self.emit_block([("sync", f_sync), ("tensor", f_pe), ("scalar", f_act),
                         ("vector", f_vec), ("gpsimd", f_gps)])

    def probe_dump(self):
        AR = self.AR

        def f_gps(eng):
            b = 0
            for p in range(NP if self.npairs is None else self.npairs):
                self.wt(eng, ("oshift", b, p))
                self.wt(eng, ("oev", b, p, 0, 15))
            cols = [(QT, 2048), (KT, 2048), (STAGE, 2048), (PT[0], 2048),
                    (PT[1], 2048), (VT, 2048), (STRIP, 1024), (NMC, 4), (VBR, 128)]
            off = 0
            for c0, w in cols:
                ins = eng.dma_start(
                    out=bass.AP(self.d_dbg, off, [[16384, 128], [1, w]]),
                    in_=AR[0:128, c0: c0 + w])
                self.tagn(self.SG, ins, 16)
                off += w
            ins = eng.dma_start(
                out=bass.AP(self.d_dbg, 128 * 16384, [[16384, 128], [1, 16384]]),
                in_=AR[0:128, BUFB: BUFB + 16384])
            self.tagn(self.SG, ins, 16)

        self.emit_block([("gpsimd", f_gps)])

    def gate_dump(self, slot=0):
        AR = self.AR

        def f_gps(eng):
            for b in range(B):
                self.wt(eng, ("mbdone", b))
                self.wt(eng, ("gbc", b))
            i1 = eng.dma_start(
                out=bass.AP(self.d_dbg, slot * 128 * 16384, [[16384, 128], [1, 1024]]),
                in_=AR[0:128, NCOLS - 1024: NCOLS])
            self.tagn(self.SG, i1, 16)
            if slot == 0 and self.stage == "fulldbg":
                self.wt(eng, ("outw", B - 1, TT - 1))
                i3 = eng.dma_start(
                    out=bass.AP(self.d_dbg, 4096, [[16384, 128], [1, 8192]]),
                    in_=AR[0:128, BUFA: BUFA + 8192])
                self.tagn(self.SG, i3, 16)
                i4 = eng.dma_start(
                    out=bass.AP(self.d_dbg, 12288, [[16384, 128], [1, 2048]]).bitcast(mybir.dt.bfloat16),
                    in_=bass.AP(self.d_mw1h, D * F, [[F, 128], [1, 4096]]))
                self.tagn(self.SG, i4, 16)

        self.emit_block([("gpsimd", f_gps)])

    def mw_wait(self):
        def f_sync(eng):
            if not self.mock:
                eng.wait_ge(self.SD.h, self.SD.n)
                eng.wait_ge(self.SG.h, self.SG.n)

        self.emit_block([("sync", f_sync)])

    def dbg_dump(self, b, what):
        AR = self.AR

        def f_gps(eng):
            if what == "attn":
                for pp in range(NP):
                    self.wt(eng, ("oev", b, pp, 0, 15))
                    self.wt(eng, ("oshift", b, pp))
            else:
                self.wt(eng, ("xtp", b, TT - 1))
            ins = eng.dma_start(
                out=bass.AP(self.d_dbg, b * 128 * 16384, [[16384, 128], [1, 16384]]),
                in_=AR[0:128, BUFB: BUFB + 16384])
            self.tag(("dbgw", b), self.SG, ins, 16)

        def f_sync(eng):
            if self.stage == "ln1" and b == 0:
                self.tag(("xt", 1), self.S_XT, eng.dma_start(
                    out=self.ar3(BUFA, [[2048, DC], [1, 2048]]),
                    in_=bass.AP(self.d_xt, D * S, [[S, 128], [128 * S, DC], [1, S]])), 16)

        self.emit_block([("gpsimd", f_gps), ("sync", f_sync)])

    def epilogue(self):
        def f_sync(eng):
            if not self.mock:
                eng.wait_ge(self.SG.h, self.SG.n)

        self.emit_block([("sync", f_sync)])


_CACHE = {}


def build_program(stage="full"):
    if stage not in _CACHE:
        _CACHE[stage] = Builder(stage).build()
    return _CACHE[stage]


def prepare_inputs(inputs):
    f = lambda k: np.ascontiguousarray(np.asarray(inputs[k], dtype=np.float32))
    x = f("L0_attn_x")
    qkv_w = f("L0_attn_qkv_w")
    qkv_b = f("L0_attn_qkv_b")
    proj_w = f("L0_attn_proj_w")
    proj_b = f("L0_attn_proj_b")
    gw1 = f("L0_ffn_gate_w1")
    gb1 = f("L0_ffn_gate_b1")
    gw2 = f("L0_ffn_gate_w2")
    gb2 = f("L0_ffn_gate_b2")
    W1 = f("L0_ffn_W1")
    b1 = f("L0_ffn_b1")
    W2 = f("L0_ffn_W2")
    b2 = f("L0_ffn_b2")
    TW1 = f("L0_ffn_TW1")
    Tb1 = f("L0_ffn_Tb1")
    TW2 = f("L0_ffn_TW2")
    Tb2 = f("L0_ffn_Tb2")

    qkvT = np.ascontiguousarray(qkv_w.T)
    wqkv = np.empty((NP, 128, 3072), np.float32)
    for p in range(NP):
        pb = np.concatenate([qkvT[:, 128 * p:128 * p + 128],
                             qkvT[:, 1024 + 128 * p:1024 + 128 * p + 128],
                             qkvT[:, 2048 + 128 * p:2048 + 128 * p + 128]], axis=1)
        wqkv[p] = pb.reshape(8, 128, 384).transpose(1, 0, 2).reshape(128, 3072)

    pwt = np.ascontiguousarray(proj_w.T.reshape(8, 128, 1024).transpose(1, 0, 2).reshape(128, 8192))
    gw1t = np.ascontiguousarray(gw1.T.reshape(8, 128, 1024).transpose(1, 0, 2).reshape(128, 8192))

    misc = np.zeros((128, MISC_W), np.float32)
    misc[:, C_ONES:C_ONES + 128] = 1.0
    misc[:, C_ID:C_ID + 128] = np.eye(128, dtype=np.float32)
    for p in range(NP):
        misc[:, C_BQKV + 3 * p + 0] = qkv_b[128 * p:128 * p + 128]
        misc[:, C_BQKV + 3 * p + 1] = qkv_b[1024 + 128 * p:1024 + 128 * p + 128]
        misc[:, C_BQKV + 3 * p + 2] = qkv_b[2048 + 128 * p:2048 + 128 * p + 128]
    misc[:, C_PJB:C_PJB + 8] = proj_b.reshape(8, 128).T
    misc[:, C_GB1:C_GB1 + 8] = gb1.reshape(8, 128).T
    misc[:, C_B1T:C_B1T + 32] = b1.reshape(32, 128).T
    misc[:, C_B2T:C_B2T + 8] = b2.reshape(8, 128).T
    misc[0:8, C_GB2] = gb2
    misc[:, C_EPS] = 1e-5
    misc[:, C_INV] = 1.0 / 2048.0
    misc[0, C_GB2R:C_GB2R + 8] = gb2
    misc[:, C_TB1:C_TB1 + 256] = np.ascontiguousarray(Tb1.T).reshape(32, 128, 8).transpose(1, 0, 2).reshape(128, 256)
    misc[:, C_TB2:C_TB2 + 64] = np.ascontiguousarray(Tb2.T).reshape(8, 128, 8).transpose(1, 0, 2).reshape(128, 64)
    misc[:, C_GW2:C_GW2 + 64] = np.ascontiguousarray(gw2.T).reshape(8, 128, 8).transpose(1, 0, 2).reshape(128, 64)

    w1t = np.ascontiguousarray(W1.T)
    tw1t = np.ascontiguousarray(TW1.transpose(0, 2, 1))
    w2t = np.ascontiguousarray(W2.T)
    tw2t = np.ascontiguousarray(TW2.transpose(0, 2, 1))

    xt_all = np.ascontiguousarray(x.transpose(0, 2, 1))

    shared = {"wqkv": wqkv, "pwt": pwt, "gw1t": gw1t, "misc": misc,
              "w1t": w1t, "tw1t": tw1t, "w2t": w2t, "tw2t": tw2t}
    in_maps = []
    for c in range(NCORE):
        m = dict(shared)
        m["xt"] = np.ascontiguousarray(xt_all[2 * c:2 * c + 2])
        in_maps.append(m)
    return in_maps


def kernel(**inputs):
    in_maps = prepare_inputs(inputs)
    nc = build_program("full")
    res = run_bass_kernel_spmd(nc, in_maps, core_ids=list(range(NCORE)))
    out = np.empty((16, 2048, 1024), np.float32)
    for c in range(NCORE):
        out[2 * c:2 * c + 2] = np.asarray(res.results[c]["outt"]).transpose(0, 2, 1)
    return out
